# version 18
# baseline (speedup 1.0000x reference)
import contextlib
import numpy as np
import ml_dtypes
import concourse.bass as bass
import concourse.mybir as mybir
from concourse.bass_utils import run_bass_kernel_spmd

F32 = mybir.dt.float32
BF16 = mybir.dt.bfloat16
AF = mybir.ActivationFunctionType
ALU = mybir.AluOpType
AX = mybir.AxisListType

MAGIC = 12582912.0
QMAXI = 1.0 / 32767.0
B, S, D = 4, 4096, 1024
NP = 8
NJ = 8
NTOK = 2048
I3, NF = 3072, 24
EPS = 1e-5


def build_nc(attn_only=False, stage=3):
    if attn_only:
        stage = 2
    nc = bass.Bass("TRN2", target_bir_lowering=False)
    xq_d = nc.dram_tensor("xq", [NTOK, D], F32, kind="ExternalInput").ap()
    xkv_d = nc.dram_tensor("xkv", [S, D], F32, kind="ExternalInput").ap()
    wq_d = nc.dram_tensor("wq", [D, D], F32, kind="ExternalInput").ap()
    wk_d = nc.dram_tensor("wk", [D, D], F32, kind="ExternalInput").ap()
    wv_d = nc.dram_tensor("wv", [D, D], F32, kind="ExternalInput").ap()
    wo_d = nc.dram_tensor("wo", [D, D], F32, kind="ExternalInput").ap()
    qb_d = nc.dram_tensor("qbias", [128, NP], F32, kind="ExternalInput").ap()
    kb_d = nc.dram_tensor("kbias", [128, NP], F32, kind="ExternalInput").ap()
    vb_d = nc.dram_tensor("vbias", [1, D], F32, kind="ExternalInput").ap()
    ob_d = nc.dram_tensor("obias", [1, D], F32, kind="ExternalInput").ap()
    msk_d = nc.dram_tensor("masks", [4, 128, 256], BF16, kind="ExternalInput").ap()
    id_d = nc.dram_tensor("ident", [128, 128], BF16, kind="ExternalInput").ap()
    sel_d = nc.dram_tensor("selm", [128, 256], F32, kind="ExternalInput").ap()
    wup_d = nc.dram_tensor("wup", [I3, D], F32, kind="ExternalInput").ap()
    wgt_d = nc.dram_tensor("wgt", [I3, D], F32, kind="ExternalInput").ap()
    wdn_d = nc.dram_tensor("wdn", [D, I3], F32, kind="ExternalInput").ap()
    out_d = nc.dram_tensor("out", [NTOK, D], F32, kind="ExternalOutput").ap()
    qbuf_d = nc.dram_tensor("qbuf", [128, NP, NTOK], BF16).ap()
    x2_d = nc.dram_tensor("x2buf", [NTOK, D], F32).ap()
    upb_d = nc.dram_tensor("upbuf", [NF, 128, NTOK], BF16).ap()
    cci_d = nc.dram_tensor("cc_in", [2, 128], F32).ap()
    cco_d = nc.dram_tensor("cc_out", [2, 128], F32).ap()
    scal_d = nc.dram_tensor("scal", [1, 16], F32).ap()

    st = contextlib.ExitStack()
    with st:
        SB = lambda n, sh, dt: st.enter_context(nc.sbuf_tensor(n, sh, dt))
        ident = SB("identb", [128, 128], BF16)
        selm = SB("selmb", [128, 256], F32)
        ones1 = SB("ones1", [1, 128], F32)
        qbias = SB("qbiasb", [128, NP], F32)
        kbias = SB("kbiasb", [128, NP], F32)
        vbrow = SB("vbrow", [1, D], F32)
        obrow = SB("obrow", [1, D], F32)
        e5 = SB("e5", [128, 1], F32)
        sc = SB("scw", [1, 512], F32)
        scb = SB("scb", [128, 8], F32)
        amx = SB("amx", [128, 104], F32)

        dS = st.enter_context(nc.semaphore("dS"))
        dG = st.enter_context(nc.semaphore("dG"))
        sV = st.enter_context(nc.semaphore("sV"))
        sS = st.enter_context(nc.semaphore("sS"))
        sT = st.enter_context(nc.semaphore("sT"))
        sC = st.enter_context(nc.semaphore("sC"))
        SEMS = {"dS": dS, "dG": dG, "sV": sV, "sS": sS, "sT": sT, "sC": sC}
        C = {k: 0 for k in SEMS}

        def W(eng, pairs):
            for name, cnt in pairs:
                if cnt:
                    eng.wait_ge(SEMS[name], cnt)

        with nc.Block() as b0:
            @b0.gpsimd
            def _(g):
                for s in SEMS.values():
                    g.sem_clear(s)

        satt = contextlib.ExitStack()
        kdat = satt.enter_context(nc.sbuf_tensor("kdat", [128, NP, S], BF16))
        vdat = satt.enter_context(nc.sbuf_tensor("vdat", [128, 32, NP * 130], BF16))
        masks = satt.enter_context(nc.sbuf_tensor("masksb", [128, 4, 256], BF16))

        with contextlib.ExitStack() as s1:
            B1 = lambda n, sh, dt: s1.enter_context(nc.sbuf_tensor(n, sh, dt))
            xst = B1("xst", [128, 2, 1024], F32)
            xln = B1("xln", [128, 2, 1024], BF16)
            xfm = B1("xfm", [128, 8, 256], BF16)
            bns = B1("bns", [128, 24], F32)
            mv = B1("mv", [128, 8], F32)
            wA = B1("wA", [128, 8, 1024], BF16)
            wB = B1("wB", [128, 8, 1024], BF16)
            qst = B1("qst", [128, 8, 256], BF16)
            ptp = [s1.enter_context(nc.psum_tensor(f"ptp{i}", [128, 1024], BF16)) for i in range(2)]
            pk = [s1.enter_context(nc.psum_tensor(f"pk{i}", [128, 512], F32)) for i in range(2)]

            with nc.Block() as block:
                @block.sync
                def _(sync):
                    sync.dma_start(out=masks[:, :, :], in_=msk_d.rearrange("t p q -> p t q")).then_inc(dS, 16)
                    sync.dma_start(out=ident[:, :], in_=id_d).then_inc(dS, 16)
                    sync.dma_start(out=selm[:, :], in_=sel_d).then_inc(dS, 16)
                    sync.dma_start(out=qbias[:, :], in_=qb_d).then_inc(dS, 16)
                    sync.dma_start(out=kbias[:, :], in_=kb_d).then_inc(dS, 16)
                    sync.dma_start(out=vbrow[:, :], in_=vb_d).then_inc(dS, 16)
                    sync.dma_start(out=obrow[:, :], in_=ob_d).then_inc(dS, 16)
                C["dS"] += 7 * 16

                @block.gpsimd
                def _(gpsimd):
                    gpsimd.dma_start(out=wA[:, :, :],
                                     in_=wk_d.rearrange("(c p) m -> p c m", p=128)).then_inc(dG, 16)
                    gpsimd.dma_start(out=wB[:, :, :],
                                     in_=wv_d.rearrange("(c p) m -> p c m", p=128)).then_inc(dG, 16)
                C["dG"] += 32
                kv_w_dG = C["dG"]

                @block.scalar
                def _(scalar):
                    scalar.preload_act_table(AF.Sqrt)

                @block.vector
                def _(vector):
                    vector.memset(ones1[:, :], 1.0)
                    vector.memset(e5[:, :], EPS)
                    vector.memset(
                        vdat[:, :, :].rearrange("p t (g c) -> p t g c", c=130)[:, :, :, 64:66], 1.0)
                    vector.maybe_drain_then_inc((sV, 1))
                C["sV"] += 1

                xst_free = 0
                pkfree = {0: ("sS", 0), 1: ("sS", 0)}
                ptp_free = {0: 0, 1: 0}
                qst_dma = 0
                wq_dG = None

                for sl in range(24):
                    kind = "kv" if sl < 16 else "q"
                    sidx = sl if sl < 16 else sl - 16

                    if kind == "q" and sidx == 0:
                        t_gate = C["sT"]

                        @block.gpsimd
                        def _(gpsimd, t_gate=t_gate):
                            gpsimd.wait_ge(sT, t_gate)
                            gpsimd.dma_start(out=wA[:, :, :],
                                             in_=wq_d.rearrange("(c p) m -> p c m", p=128)).then_inc(dG, 16)
                        C["dG"] += 16
                        wq_dG = C["dG"]

                    @block.sync
                    def _(sync, kind=kind, sidx=sidx, need_v=xst_free):
                        if need_v:
                            sync.wait_ge(sV, need_v)
                        src = xkv_d if kind == "kv" else xq_d
                        sync.dma_start(
                            out=xst[:, :, :],
                            in_=src[sidx * 256:(sidx + 1) * 256, :].rearrange("(t p) d -> p t d", p=128),
                        ).then_inc(dS, 16)
                    C["dS"] += 16
                    x_dma = C["dS"]

                    @block.vector
                    def _(vector, x_dma=x_dma):
                        vector.wait_ge(dS, x_dma)
                        vector.bn_stats(bns[:, 0:6], xst[:, 0, 0:512])
                        vector.bn_stats(bns[:, 6:12], xst[:, 0, 512:1024])
                        vector.bn_stats(bns[:, 12:18], xst[:, 1, 0:512])
                        vector.bn_stats(bns[:, 18:24], xst[:, 1, 512:1024])
                        vector.drain()
                        vector.bn_aggr(mv[:, 0:2], bns[:, 0:12])
                        vector.bn_aggr(mv[:, 2:4], bns[:, 12:24])
                        vector.drain()
                        vector.tensor_scalar_add(mv[:, 4:5], mv[:, 1:2], EPS)
                        vector.tensor_scalar_add(mv[:, 5:6], mv[:, 3:4], EPS)
                        vector.maybe_drain_then_inc((sV, 1))
                    C["sV"] += 1

                    @block.scalar
                    def _(scalar, need=C["sV"]):
                        scalar.wait_ge(sV, need)
                        scalar.sqrt(mv[:, 6:8], mv[:, 4:6])
                        scalar.maybe_drain_then_inc((sS, 1))
                    C["sS"] += 1

                    @block.vector
                    def _(vector, need=C["sS"]):
                        vector.wait_ge(sS, need)
                        vector.reciprocal(mv[:, 6:8], mv[:, 6:8])
                        vector.drain()
                        vector.tensor_scalar(xln[:, 0, :], xst[:, 0, :], mv[:, 0:1], mv[:, 6:7],
                                             ALU.subtract, ALU.mult)
                        vector.tensor_scalar(xln[:, 1, :], xst[:, 1, :], mv[:, 2:3], mv[:, 7:8],
                                             ALU.subtract, ALU.mult)
                        vector.maybe_drain_then_inc((sV, 1))
                    C["sV"] += 1
                    xst_free = C["sV"]
                    norm_done = C["sV"]

                    for cg in range(2):
                        par = cg

                        @block.tensor
                        def _(tensor, cg=cg, par=par, need_v=norm_done, pfree=ptp_free[par]):
                            tensor.wait_ge(sV, max(need_v, pfree))
                            for cl in range(4):
                                c = 4 * cg + cl
                                for t in range(2):
                                    ins = tensor.transpose(
                                        ptp[par][:, (cl * 2 + t) * 128:(cl * 2 + t + 1) * 128],
                                        xln[:, t, c * 128:(c + 1) * 128], ident[:, :])
                            ins.then_inc(sT, 1)
                        C["sT"] += 1

                        @block.vector
                        def _(vector, cg=cg, par=par, need=C["sT"]):
                            vector.wait_ge(sT, need)
                            vector.tensor_copy(
                                xfm[:, 4 * cg:4 * cg + 4, :],
                                ptp[par][:, :].rearrange("p (cl q) -> p cl q", cl=4))
                            vector.maybe_drain_then_inc((sV, 1))
                        C["sV"] += 1
                        ptp_free[par] = C["sV"]
                    xfm_done = C["sV"]

                    if kind == "kv":
                        for g in range(NP):
                            par = g % 2
                            fsem, fcnt = pkfree[par]

                            @block.tensor
                            def _(tensor, g=g, par=par, need_v=xfm_done, fsem=fsem, fcnt=fcnt,
                                  first=(sl == 0 and g == 0), kv_w_dG=kv_w_dG):
                                tensor.wait_ge(sV, need_v)
                                W(tensor, [(fsem, fcnt)])
                                if first:
                                    tensor.wait_ge(dG, kv_w_dG)
                                for c in range(8):
                                    ins = tensor.matmul(pk[par][:, 0:256],
                                                        wA[:, c, g * 128:(g + 1) * 128],
                                                        xfm[:, c, :], start=(c == 0), stop=(c == 7))
                                ins.then_inc(sT, 1)
                            C["sT"] += 1

                            @block.vector
                            def _(vector, g=g, par=par, sidx=sidx, need=C["sT"]):
                                vector.wait_ge(sT, need)
                                vector.tensor_scalar(kdat[:, g, sidx * 256:(sidx + 1) * 256],
                                                     pk[par][:, 0:256], kbias[:, g:g + 1], None,
                                                     ALU.add)
                                vector.maybe_drain_then_inc((sV, 1))
                            C["sV"] += 1
                            pkfree[par] = ("sV", C["sV"])

                        for tt in range(2):
                            for half in range(2):
                                par = (tt * 2 + half) % 2
                                fsem, fcnt = pkfree[par]

                                @block.tensor
                                def _(tensor, tt=tt, half=half, par=par, need_v=xfm_done,
                                      fsem=fsem, fcnt=fcnt):
                                    tensor.wait_ge(sV, need_v)
                                    W(tensor, [(fsem, fcnt)])
                                    for c in range(8):
                                        tensor.matmul(pk[par][:, :], xfm[:, c, tt * 128:(tt + 1) * 128],
                                                      wB[:, c, half * 512:(half + 1) * 512],
                                                      start=(c == 0), stop=False)
                                    tensor.matmul(pk[par][:, :], ones1[0:1, :],
                                                  vbrow[0:1, half * 512:(half + 1) * 512],
                                                  start=False, stop=True).then_inc(sT, 1)
                                C["sT"] += 1

                                @block.vector
                                def _(vector, tt=tt, half=half, par=par, sidx=sidx, need=C["sT"]):
                                    vector.wait_ge(sT, need)
                                    T = sidx * 2 + tt
                                    pv = pk[par][:, :].rearrange("p (g two c) -> p g two c", two=2, c=64)
                                    ov = vdat[:, T:T + 1, :].rearrange(
                                        "p one (g c) -> p (one g) c", c=130)[:, half * 4:(half + 1) * 4, :]
                                    vector.tensor_copy(ov[:, :, 0:64], pv[:, :, 0, :])
                                    vector.tensor_copy(ov[:, :, 66:130], pv[:, :, 1, :])
                                    vector.maybe_drain_then_inc((sV, 1))
                                C["sV"] += 1
                                pkfree[par] = ("sV", C["sV"])
                    else:
                        for g in range(NP):
                            par = g % 2
                            fsem, fcnt = pkfree[par]

                            @block.tensor
                            def _(tensor, g=g, par=par, need_v=xfm_done, fsem=fsem, fcnt=fcnt,
                                  first=(sidx == 0 and g == 0), wq_dG=wq_dG):
                                tensor.wait_ge(sV, need_v)
                                W(tensor, [(fsem, fcnt)])
                                if first:
                                    tensor.wait_ge(dG, wq_dG)
                                for c in range(8):
                                    ins = tensor.matmul(pk[par][:, 0:256],
                                                        wA[:, c, g * 128:(g + 1) * 128],
                                                        xfm[:, c, :], start=(c == 0), stop=(c == 7))
                                ins.then_inc(sT, 1)
                            C["sT"] += 1

                            @block.vector
                            def _(vector, g=g, par=par, need=C["sT"], qfree=(C["dS"] if (qst_dma and g == 0) else 0)):
                                vector.wait_ge(sT, need)
                                if qfree:
                                    vector.wait_ge(dS, qfree)
                                vector.tensor_scalar(qst[:, g, :], pk[par][:, 0:256],
                                                     qbias[:, g:g + 1], None, ALU.add)
                                vector.maybe_drain_then_inc((sV, 1))
                            C["sV"] += 1
                            pkfree[par] = ("sV", C["sV"])

                        @block.sync
                        def _(sync, sidx=sidx, need=C["sV"]):
                            sync.wait_ge(sV, need)
                            sync.dma_start(out=qbuf_d[:, :, sidx * 256:(sidx + 1) * 256],
                                           in_=qst[:, :, :]).then_inc(dS, 16)
                        C["dS"] += 16
                        qst_dma = C["dS"]

            p1_dS = C["dS"]

        if stage == 1:
            with nc.Block() as block:
                @block.gpsimd
                def _(g, sSn=C["sS"], sVn=C["sV"], dSn=C["dS"]):
                    g.wait_ge(sS, sSn)
                    g.wait_ge(sV, sVn)
                    g.wait_ge(dS, dSn)
                    g.dma_start(out=out_d[0:128, :], in_=kdat[:, 0, 0:1024]).then_inc(dG, 16)
                    g.dma_start(out=out_d[128:256, :], in_=kdat[:, 7, 3072:4096]).then_inc(dG, 16)
                    g.dma_start(out=out_d[256:384, :], in_=qbuf_d[:, 0, 0:1024]).then_inc(dG, 16)
                    g.dma_start(out=out_d[384:512, :], in_=qbuf_d[:, 7, 1024:2048]).then_inc(dG, 16)
                    g.dma_start(out=out_d[512:640, :], in_=vdat[:, 0, 0:1024]).then_inc(dG, 16)
                    g.dma_start(out=out_d[640:768, :], in_=vdat[:, 31, 0:1024]).then_inc(dG, 16)
                    g.wait_ge(dG, C["dG"] + 96)
            satt.close()
            return nc

        with contextlib.ExitStack() as s2:
            B2 = lambda n, sh, dt: s2.enter_context(nc.sbuf_tensor(n, sh, dt))
            woS = B2("woS", [128, 8, 1024], BF16)
            qj = [B2(f"qj{i}", [128, NP, 256], BF16) for i in range(2)]
            Pb = [B2(f"Pb{i}", [128, 2048], BF16) for i in range(2)]
            attn = B2("attn", [128, 8, 256], BF16)
            lrow = B2("lrow", [128, 256], F32)
            lbc = B2("lbc", [128, 256], F32)
            lbc2 = B2("lbc2", [128, 256], F32)
            xres = B2("xres", [128, 2, 1024], F32)
            ps_s = s2.enter_context(nc.psum_tensor("pss", [128, 2048], F32))
            EO = [s2.enter_context(nc.psum_tensor(f"EO{i}", [128, 512], F32)) for i in range(2)]
            pbb = s2.enter_context(nc.psum_tensor("pbb", [128, 512], F32))
            py = s2.enter_context(nc.psum_tensor("py", [128, 512], F32))

            with nc.Block() as block:
                @block.gpsimd
                def _(gpsimd):
                    gpsimd.dma_start(out=woS[:, :, :],
                                     in_=wo_d.rearrange("(c p) m -> p c m", p=128)).then_inc(dG, 16)
                C["dG"] += 16
                wo_dG = C["dG"]

                @block.scalar
                def _(scalar):
                    scalar.preload_act_table(AF.Exp)

                @block.vector
                def _(vector):
                    vector.memset(lrow[:, :], 0.0)
                    vector.memset(EO[0][64:96, 0:256], 1.0)
                    vector.memset(EO[1][64:96, 0:256], 1.0)
                    vector.maybe_drain_then_inc((sV, 1))
                C["sV"] += 1

                exp_hist = 0
                pv_hist = {0: 0, 1: 0}
                evac_hist = {0: C["sV"], 1: C["sV"]}
                lrow_hist = 0
                lbc_hist = 0
                oproj_hist = 0
                py_free = 0
                xres_dma = 0
                gseq = 0

                for j in range(NJ):
                    qj_cur = qj[j % 2]

                    @block.sync
                    def _(sync, j=j, need=C["dS"], xfree=py_free):
                        sync.wait_ge(dS, need)
                        if xfree:
                            sync.wait_ge(sV, xfree)
                        sync.dma_start(out=qj_cur[:, :, :],
                                       in_=qbuf_d[:, :, j * 256:(j + 1) * 256]).then_inc(dS, 16)
                        sync.dma_start(
                            out=xres[:, :, :],
                            in_=xq_d[j * 256:(j + 1) * 256, :].rearrange("(t p) d -> p t d", p=128),
                        ).then_inc(dS, 16)
                    C["dS"] += 32
                    qj_ready = C["dS"]
                    xres_dma = C["dS"]

                    for g in range(NP):
                        ngrp2 = j + 1
                        gp = gseq % 2

                        def emit_pv(kg2, par2, p_gate, gp, ngrp2, g):
                            @block.tensor
                            def _(tensor, kg2=kg2, par2=par2, p_gate=p_gate, gp=gp,
                                  ngrp2=ngrp2, g=g,
                                  ev_free=(evac_hist[gp] if kg2 == 0 else 0)):
                                W(tensor, [p_gate])
                                if ev_free:
                                    tensor.wait_ge(sV, ev_free)
                                for kt in range(4):
                                    T = 4 * kg2 + kt
                                    first = (kg2 == 0 and kt == 0)
                                    last = (kg2 == ngrp2 - 1 and kt == 3)
                                    mmO = lambda s, p: tensor.matmul(
                                        EO[gp][0:128, 256:512],
                                        vdat[:, T, g * 130 + 2:g * 130 + 130],
                                        Pb[par2][:, 1024 + kt * 256:1024 + (kt + 1) * 256],
                                        start=s, stop=p)
                                    mmE = lambda: tensor.matmul(
                                        EO[gp][0:65, 0:256],
                                        vdat[:, T, g * 130:g * 130 + 65],
                                        Pb[par2][:, kt * 256:(kt + 1) * 256],
                                        start=False, stop=False)
                                    if last:
                                        mmE()
                                        ins = mmO(False, True)
                                    else:
                                        mmO(first, False)
                                        ins = mmE()
                                ins.then_inc(sT, 1)
                            C["sT"] += 1
                            pv_hist[par2] = C["sT"]

                        pend = None
                        for kg2 in range(ngrp2):
                            par2 = kg2 % 2

                            @block.tensor
                            def _(tensor, j=j, g=g, kg2=kg2, qj_cur=qj_cur, need_q=qj_ready,
                                  need_wo=(wo_dG if (j == 0 and g == 0 and kg2 == 0) else 0),
                                  exp_free=exp_hist):
                                tensor.wait_ge(dS, need_q)
                                if need_wo:
                                    tensor.wait_ge(dG, need_wo)
                                if exp_free:
                                    tensor.wait_ge(sS, exp_free)
                                for kt in range(4):
                                    T = 4 * kg2 + kt
                                    for hh in range(2):
                                        ins = tensor.matmul(
                                            ps_s[:, hh * 1024 + kt * 256:hh * 1024 + (kt + 1) * 256],
                                            kdat[hh * 64:(hh + 1) * 64, g, T * 128:(T + 1) * 128],
                                            qj_cur[hh * 64:(hh + 1) * 64, g, :],
                                            start=True, stop=True)
                                ins.then_inc(sT, 1)
                            C["sT"] += 1

                            @block.scalar
                            def _(scalar, par2=par2, need=C["sT"], pv_free=pv_hist[par2]):
                                scalar.wait_ge(sT, max(need, pv_free))
                                scalar.activation(Pb[par2][:, 0:1024], ps_s[:, 0:1024], AF.Exp,
                                                  bias=0.0, scale=0.125)
                                scalar.activation(Pb[par2][:, 1024:2048], ps_s[:, 1024:2048], AF.Exp,
                                                  bias=0.0, scale=0.125)
                                scalar.maybe_drain_then_inc((sS, 1))
                            C["sS"] += 1
                            exp_hist = C["sS"]

                            if kg2 == j:
                                @block.vector
                                def _(vector, par2=par2, need=C["sS"]):
                                    vector.wait_ge(sS, need)
                                    for hh in range(2):
                                        pv3 = Pb[par2][:, hh * 1024:(hh + 1) * 1024].rearrange(
                                            "p (t q) -> p t q", q=256)
                                        vector.tensor_tensor(pv3, pv3, masks[:, :, :], ALU.mult)
                                    vector.maybe_drain_then_inc((sV, 1))
                                C["sV"] += 1
                                p_gate = ("sV", C["sV"])
                            else:
                                p_gate = ("sS", C["sS"])

                            if pend is not None:
                                emit_pv(*pend)
                            pend = (kg2, par2, p_gate, gp, ngrp2, g)
                        emit_pv(*pend)

                        @block.vector
                        def _(vector, gp=gp, need=C["sT"], lfree=lrow_hist):
                            vector.wait_ge(sT, max(need, lfree))
                            vector.tensor_copy(lrow[64:96, :], EO[gp][64:96, 0:256])
                            vector.tensor_copy(lrow[32:64, :], EO[gp][32:64, 256:512])
                            vector.drain()
                            vector.tensor_scalar_max(lrow[32:64, :], lrow[32:64, :], 1e-35)
                            vector.tensor_scalar_max(lrow[64:96, :], lrow[64:96, :], 1e-35)
                            vector.drain()
                            vector.reciprocal(lrow[32:64, :], lrow[32:64, :])
                            vector.reciprocal(lrow[64:96, :], lrow[64:96, :])
                            vector.maybe_drain_then_inc((sV, 1))
                        C["sV"] += 1

                        @block.tensor
                        def _(tensor, need=C["sV"], lfree=lbc_hist):
                            tensor.wait_ge(sV, max(need, lfree))
                            tensor.matmul(pbb[:, 0:256], selm[:, 0:128], lrow[:, :],
                                          start=True, stop=True)
                            tensor.matmul(pbb[:, 256:512], selm[:, 128:256], lrow[:, :],
                                          start=True, stop=True).then_inc(sT, 1)
                        C["sT"] += 1
                        lrow_hist = C["sT"]

                        @block.vector
                        def _(vector, g=g, gp=gp, need=C["sT"], op_gate=oproj_hist):
                            vector.wait_ge(sT, max(need, op_gate))
                            vector.tensor_copy(lbc[:, :], pbb[:, 0:256])
                            vector.tensor_copy(lbc2[:, :], pbb[:, 256:512])
                            vector.drain()
                            vector.tensor_tensor(attn[0:64, g, :], EO[gp][0:64, 0:256],
                                                 lbc[0:64, :], ALU.mult)
                            vector.tensor_tensor(attn[64:128, g, :], EO[gp][64:128, 256:512],
                                                 lbc2[64:128, :], ALU.mult)
                            vector.maybe_drain_then_inc((sV, 1))
                        C["sV"] += 1
                        lbc_hist = C["sV"]
                        evac_hist[gp] = C["sV"]
                        gseq += 1

                    attn_done = C["sV"]
                    for stt in range(2):
                        for half in range(2):
                            @block.tensor
                            def _(tensor, stt=stt, half=half, need_v=attn_done,
                                  need_x=xres_dma, pfree=py_free):
                                tensor.wait_ge(sV, max(need_v, pfree))
                                tensor.wait_ge(dS, need_x)
                                for c in range(8):
                                    tensor.matmul(py[:, :], attn[:, c, stt * 128:(stt + 1) * 128],
                                                  woS[:, c, half * 512:(half + 1) * 512],
                                                  start=(c == 0), stop=False)
                                tensor.matmul(py[:, :], ones1[0:1, :],
                                              obrow[0:1, half * 512:(half + 1) * 512],
                                              start=False, stop=True).then_inc(sT, 1)
                            C["sT"] += 1

                            @block.vector
                            def _(vector, stt=stt, half=half, need=C["sT"]):
                                vector.wait_ge(sT, need)
                                vector.tensor_tensor(xres[:, stt, half * 512:(half + 1) * 512],
                                                     py[:, :],
                                                     xres[:, stt, half * 512:(half + 1) * 512],
                                                     ALU.add)
                                vector.maybe_drain_then_inc((sV, 1))
                            C["sV"] += 1
                            py_free = C["sV"]

                        @block.sync
                        def _(sync, j=j, stt=stt, need=C["sV"]):
                            sync.wait_ge(sV, need)
                            dst = out_d if attn_only else x2_d
                            sync.dma_start(
                                out=dst[j * 256 + stt * 128:j * 256 + (stt + 1) * 128, :],
                                in_=xres[:, stt, :]).then_inc(dS, 16)
                        C["dS"] += 16
                    oproj_hist = C["sT"]

            p2_dS = C["dS"]

        satt.close()

        if attn_only:
            with nc.Block() as block:
                @block.sync
                def _(sync):
                    sync.wait_ge(dS, p2_dS)
            return nc

        def ar_round(block, src_cols):
            @block.sync
            def _(sync, src_cols=src_cols, need=C["sV"]):
                sync.wait_ge(sV, need)
                for i in range(2):
                    col = src_cols[i] if i < len(src_cols) else src_cols[0]
                    sync.dma_start(out=cci_d[i:i + 1, :].rearrange("one p -> p one"),
                                   in_=col).then_inc(dS, 16)
            C["dS"] += 32

            @block.gpsimd
            def _(gpsimd, need=C["dS"]):
                gpsimd.wait_ge(dS, need)
                gpsimd.collective_compute(
                    "AllReduce", ALU.max, replica_groups=[list(range(8))],
                    ins=[cci_d[0:2, :]], outs=[cco_d[0:2, :]],
                ).then_inc(sC, 1)
            C["sC"] += 1

            @block.sync
            def _(sync, need=C["sC"]):
                sync.wait_ge(sC, need)
                sync.dma_start(out=sc[0:1, 128:256], in_=cco_d[0:1, :]).then_inc(dS, 16)
                sync.dma_start(out=sc[0:1, 256:384], in_=cco_d[1:2, :]).then_inc(dS, 16)
            C["dS"] += 32

            @block.vector
            def _(vector, n=len(src_cols), need=C["dS"]):
                vector.wait_ge(dS, need)
                vector.tensor_reduce(sc[0:1, 48:49], sc[0:1, 128:256], AX.X, ALU.max)
                if n == 2:
                    vector.tensor_reduce(sc[0:1, 49:50], sc[0:1, 256:384], AX.X, ALU.max)
                vector.maybe_drain_then_inc((sV, 1))
            C["sV"] += 1

        def bcast_scalars(block, n, scb_cols):
            @block.sync
            def _(sync, n=n, need=C["sV"]):
                sync.wait_ge(sV, need)
                sync.dma_start(out=scal_d[0:1, 0:n], in_=sc[0:1, 32:32 + n]).then_inc(dS, 16)
            C["dS"] += 16

            @block.sync
            def _(sync, n=n, scb_cols=scb_cols, need=C["dS"]):
                sync.wait_ge(dS, need)
                for i in range(n):
                    sync.dma_start(out=scb[:, scb_cols[i]:scb_cols[i] + 1],
                                   in_=scal_d[0:1, i:i + 1].broadcast_to([128, 1])).then_inc(dS, 16)
            C["dS"] += 16 * n

        with contextlib.ExitStack() as s3o:
            B3 = lambda n, sh, dt: s3o.enter_context(nc.sbuf_tensor(n, sh, dt))
            scrA = B3("scrA", [128, 2048], F32)
            scrB = B3("scrB", [128, 2048], F32)
            sq = B3("sq", [128, 32], F32)
            sqi = B3("sqi", [128, 32], F32)
            pg3 = [s3o.enter_context(nc.psum_tensor(f"pg3{i}", [128, 512], F32)) for i in range(2)]
            ptp3 = [s3o.enter_context(nc.psum_tensor(f"ptp3{i}", [128, 1024], BF16)) for i in range(2)]

            with contextlib.ExitStack() as s3b:
                Xfm = s3b.enter_context(nc.sbuf_tensor("Xfm", [128, 8, NTOK], BF16))
                wgrid = s3b.enter_context(nc.sbuf_tensor("wgrid", [128, 1024], BF16))

                with contextlib.ExitStack() as s3a:
                    xln2 = s3a.enter_context(nc.sbuf_tensor("xln2", [128, 16, 1024], F32))
                    xst2 = s3a.enter_context(nc.sbuf_tensor("xst2", [128, 2, 1024], F32))
                    bns2 = s3a.enter_context(nc.sbuf_tensor("bns2", [128, 12], F32))
                    mv2 = s3a.enter_context(nc.sbuf_tensor("mv2", [128, 4], F32))

                    with nc.Block() as block:
                        @block.scalar
                        def _(scalar):
                            scalar.preload_act_table(AF.Sqrt)

                        x2_free = {0: 0, 1: 0}
                        for t in range(16):
                            par = t % 2

                            @block.sync
                            def _(sync, t=t, par=par, need=(p2_dS if t == 0 else 0),
                                  free=x2_free[par]):
                                if need:
                                    sync.wait_ge(dS, need)
                                if free:
                                    sync.wait_ge(sV, free)
                                sync.dma_start(out=xst2[:, par, :],
                                               in_=x2_d[t * 128:(t + 1) * 128, :]).then_inc(dS, 16)
                            C["dS"] += 16

                            @block.vector
                            def _(vector, par=par, need=C["dS"]):
                                vector.wait_ge(dS, need)
                                vector.bn_stats(bns2[:, 0:6], xst2[:, par, 0:512])
                                vector.bn_stats(bns2[:, 6:12], xst2[:, par, 512:1024])
                                vector.drain()
                                vector.bn_aggr(mv2[:, 0:2], bns2[:, 0:12])
                                vector.drain()
                                vector.tensor_scalar_add(mv2[:, 2:3], mv2[:, 1:2], EPS)
                                vector.maybe_drain_then_inc((sV, 1))
                            C["sV"] += 1

                            @block.scalar
                            def _(scalar, need=C["sV"]):
                                scalar.wait_ge(sV, need)
                                scalar.sqrt(mv2[:, 3:4], mv2[:, 2:3])
                                scalar.maybe_drain_then_inc((sS, 1))
                            C["sS"] += 1

                            @block.vector
                            def _(vector, t=t, par=par, need=C["sS"]):
                                vector.wait_ge(sS, need)
                                vector.reciprocal(mv2[:, 3:4], mv2[:, 3:4])
                                vector.drain()
                                vector.tensor_scalar(xln2[:, t, :], xst2[:, par, :], mv2[:, 0:1],
                                                     mv2[:, 3:4], ALU.subtract, ALU.mult)
                                vector.drain()
                                vector.tensor_reduce(amx[:, t:t + 1], xln2[:, t, :], AX.X, ALU.max,
                                                     apply_absolute_value=True)
                                vector.maybe_drain_then_inc((sV, 1))
                            C["sV"] += 1
                            x2_free[par] = C["sV"]

                        @block.vector
                        def _(vector, need=C["sV"]):
                            vector.wait_ge(sV, need)
                            vector.tensor_reduce(amx[:, 16:17], amx[:, 0:16], AX.X, ALU.max)
                            vector.maybe_drain_then_inc((sV, 1))
                        C["sV"] += 1
                        ar_round(block, [amx[:, 16:17]])

                        @block.vector
                        def _(vector, need=C["sV"]):
                            vector.wait_ge(sV, need)
                            vector.tensor_scalar(sc[0:1, 0:1], sc[0:1, 48:49], QMAXI, 1e-12,
                                                 ALU.mult, ALU.max)
                            vector.drain()
                            vector.reciprocal(sc[0:1, 32:33], sc[0:1, 0:1])
                            vector.maybe_drain_then_inc((sV, 1))
                        C["sV"] += 1
                        bcast_scalars(block, 1, [0])

                        grid_free = 0
                        ptp3_free = {0: 0, 1: 0}
                        for t in range(16):
                            par = t % 2

                            @block.vector
                            def _(vector, t=t, need=C["dS"], tfree=grid_free):
                                vector.wait_ge(dS, need)
                                if tfree:
                                    vector.wait_ge(sT, tfree)
                                vector.tensor_scalar(scrA[:, 0:1024], xln2[:, t, :], scb[:, 0:1],
                                                     MAGIC, ALU.mult, ALU.add)
                                vector.drain()
                                vector.tensor_scalar(wgrid[:, :], scrA[:, 0:1024], MAGIC, None,
                                                     ALU.subtract)
                                vector.maybe_drain_then_inc((sV, 1))
                            C["sV"] += 1

                            @block.tensor
                            def _(tensor, par=par, need=C["sV"], pfree=ptp3_free[par]):
                                tensor.wait_ge(sV, max(need, pfree))
                                for c in range(8):
                                    ins = tensor.transpose(ptp3[par][:, c * 128:(c + 1) * 128],
                                                           wgrid[:, c * 128:(c + 1) * 128], ident[:, :])
                                ins.then_inc(sT, 1)
                            C["sT"] += 1
                            grid_free = C["sT"]

                            @block.vector
                            def _(vector, t=t, par=par, need=C["sT"]):
                                vector.wait_ge(sT, need)
                                vector.tensor_copy(
                                    Xfm[:, :, t * 128:(t + 1) * 128],
                                    ptp3[par][:, :].rearrange("p (c q) -> p c q", c=8))
                                vector.maybe_drain_then_inc((sV, 1))
                            C["sV"] += 1
                            ptp3_free[par] = C["sV"]

                gate = s3b.enter_context(nc.sbuf_tensor("gate", [128, NF, NTOK], BF16))
                scrC = s3b.enter_context(nc.sbuf_tensor("scrC", [128, 2048], F32))
                scrD = s3b.enter_context(nc.sbuf_tensor("scrD", [128, 2048], F32))
                wst = [s3b.enter_context(nc.sbuf_tensor(f"wst{i}", [128, 1024], F32)) for i in range(2)]
                wT = [s3b.enter_context(nc.sbuf_tensor(f"wT{i}", [128, 8, 128], BF16)) for i in range(2)]
                upst = [s3b.enter_context(nc.sbuf_tensor(f"upst{i}", [128, NTOK], BF16)) for i in range(2)]

                with nc.Block() as block:
                    state = {"dq_done": 0, "w_dma": {0: 0, 1: 0},
                             "pg_free": {0: 0, 1: 0}, "up_dma": {0: 0, 1: 0},
                             "ptp3_free": {0: C["sT"], 1: C["sT"]}}

                    chain_state = {"scrD_free": 0}

                    def emit_gate_chain(f):
                        @block.vector
                        def _(vector, f=f, need=C["dS"]):
                            vector.wait_ge(dS, need)
                            vector.tensor_scalar(scrC[:, :], gate[:, f, :], scb[:, 1:2], MAGIC,
                                                 ALU.mult, ALU.add)
                            vector.drain()
                            vector.tensor_scalar(gate[:, f, :], scrC[:, :], MAGIC, None,
                                                 ALU.subtract)
                            vector.maybe_drain_then_inc((sV, 1))
                        C["sV"] += 1

                        @block.scalar
                        def _(scalar, f=f, need=C["sV"], sfree=chain_state["scrD_free"]):
                            scalar.wait_ge(sV, max(need, sfree))
                            scalar.activation(scrD[:, :], gate[:, f, :], AF.Sigmoid,
                                              bias=0.0, scale=scb[:, 2:3])
                            scalar.maybe_drain_then_inc((sS, 1))
                        C["sS"] += 1

                        @block.vector
                        def _(vector, f=f, need=C["sS"]):
                            vector.wait_ge(sS, need)
                            vector.tensor_scalar(scrC[:, :], scrD[:, :], 65536.0, MAGIC,
                                                 ALU.mult, ALU.add)
                            vector.drain()
                            vector.tensor_scalar(scrD[:, :], scrC[:, :], MAGIC, 65535.0,
                                                 ALU.subtract, ALU.min)
                            vector.drain()
                            vector.tensor_tensor(gate[:, f, :], gate[:, f, :], scrD[:, :], ALU.mult)
                            vector.drain()
                            vector.tensor_reduce(amx[:, f:f + 1], gate[:, f, :], AX.X, ALU.max,
                                                 apply_absolute_value=True)
                            vector.maybe_drain_then_inc((sV, 1))
                        C["sV"] += 1
                        chain_state["scrD_free"] = C["sV"]

                    def ffn_mm(which, wsrc_d, amx_base):
                        for f in range(NF):
                            par = f % 2

                            @block.sync
                            def _(sync, f=f, par=par, wsrc_d=wsrc_d, free=state["dq_done"]):
                                if free:
                                    sync.wait_ge(sV, free)
                                sync.dma_start(out=wst[par][:, :],
                                               in_=wsrc_d[f * 128:(f + 1) * 128, :]).then_inc(dS, 16)
                            C["dS"] += 16

                            @block.vector
                            def _(vector, par=par, need=C["dS"], tfree=state["ptp3_free"][par]):
                                vector.wait_ge(dS, need)
                                if tfree:
                                    vector.wait_ge(sT, tfree)
                                w3 = wst[par][:, :].rearrange("p (b c) -> p b c", c=32)
                                a3 = scrA[:, 0:1024].rearrange("p (b c) -> p b c", c=32)
                                vector.tensor_reduce(sq[:, :].rearrange("p (b o) -> p b o", o=1),
                                                     w3, AX.X, ALU.max, apply_absolute_value=True)
                                vector.drain()
                                vector.tensor_scalar(sq[:, :], sq[:, :], 1.0 / 7.0, 1e-12,
                                                     ALU.mult, ALU.max)
                                vector.drain()
                                vector.reciprocal(sqi[:, :], sq[:, :])
                                vector.drain()
                                vector.tensor_tensor(a3, w3, sqi[:, :].rearrange("p (b o) -> p b o", o=1)
                                                     .broadcast_to([128, 32, 32]), ALU.mult)
                                vector.drain()
                                vector.tensor_scalar(scrA[:, 0:1024], scrA[:, 0:1024], MAGIC, MAGIC,
                                                     ALU.add, ALU.subtract)
                                vector.drain()
                                vector.tensor_tensor(wgrid[:, :].rearrange("p (b c) -> p b c", c=32),
                                                     a3, sq[:, :].rearrange("p (b o) -> p b o", o=1)
                                                     .broadcast_to([128, 32, 32]), ALU.mult)
                                vector.maybe_drain_then_inc((sV, 1))
                            C["sV"] += 1

                            @block.tensor
                            def _(tensor, par=par, need=C["sV"], wfree=state["dq_done"]):
                                tensor.wait_ge(sV, need)
                                if wfree:
                                    tensor.wait_ge(sV, wfree)
                                for c in range(8):
                                    ins = tensor.transpose(ptp3[par][:, c * 128:(c + 1) * 128],
                                                           wgrid[:, c * 128:(c + 1) * 128], ident[:, :])
                                ins.then_inc(sT, 1)
                            C["sT"] += 1
                            state["ptp3_free"][par] = C["sT"]

                            @block.vector
                            def _(vector, par=par, need=C["sT"]):
                                vector.wait_ge(sT, need)
                                vector.tensor_copy(wT[par][:, :, :],
                                                   ptp3[par][:, :].rearrange("p (c q) -> p c q", c=8))
                                vector.maybe_drain_then_inc((sV, 1))
                            C["sV"] += 1
                            state["dq_done"] = C["sV"]

                            for sp in range(4):
                                ppar = sp % 2

                                @block.tensor
                                def _(tensor, par=par, sp=sp, ppar=ppar, need=state["dq_done"],
                                      pfree=state["pg_free"][ppar]):
                                    tensor.wait_ge(sV, max(need, pfree))
                                    for c in range(8):
                                        ins = tensor.matmul(pg3[ppar][:, :], wT[par][:, c, :],
                                                            Xfm[:, c, sp * 512:(sp + 1) * 512],
                                                            start=(c == 0), stop=(c == 7))
                                    ins.then_inc(sT, 1)
                                C["sT"] += 1

                                @block.vector
                                def _(vector, f=f, sp=sp, ppar=ppar, which=which, need=C["sT"],
                                      ufree=(C["dS"] if (which == "up" and sp == 0 and state["up_dma"][f % 2]) else 0)):
                                    vector.wait_ge(sT, need)
                                    if which == "gate":
                                        vector.tensor_copy(gate[:, f, sp * 512:(sp + 1) * 512],
                                                           pg3[ppar][:, :])
                                    else:
                                        if ufree:
                                            vector.wait_ge(dS, ufree)
                                        vector.tensor_copy(upst[f % 2][:, sp * 512:(sp + 1) * 512],
                                                           pg3[ppar][:, :])
                                    vector.maybe_drain_then_inc((sV, 1))
                                C["sV"] += 1
                                state["pg_free"][ppar] = C["sV"]

                            if which == "gate":
                                @block.vector
                                def _(vector, f=f, need=C["sV"]):
                                    vector.wait_ge(sV, need)
                                    vector.tensor_reduce(amx[:, amx_base + f:amx_base + f + 1],
                                                         gate[:, f, :], AX.X, ALU.max,
                                                         apply_absolute_value=True)
                                    vector.maybe_drain_then_inc((sV, 1))
                                C["sV"] += 1
                            else:
                                @block.vector
                                def _(vector, f=f, need=C["sV"]):
                                    vector.wait_ge(sV, need)
                                    vector.tensor_reduce(amx[:, amx_base + f:amx_base + f + 1],
                                                         upst[f % 2][:, :], AX.X, ALU.max,
                                                         apply_absolute_value=True)
                                    vector.maybe_drain_then_inc((sV, 1))
                                C["sV"] += 1

                                @block.sync
                                def _(sync, f=f, need=C["sV"]):
                                    sync.wait_ge(sV, need)
                                    sync.dma_start(out=upb_d[f, :, :],
                                                   in_=upst[f % 2][:, :]).then_inc(dS, 16)
                                C["dS"] += 16
                                state["up_dma"][f % 2] = C["dS"]
                                emit_gate_chain(f)

                    ffn_mm("gate", wgt_d, 0)

                    @block.vector
                    def _(vector, need=C["sV"]):
                        vector.wait_ge(sV, need)
                        vector.tensor_reduce(amx[:, 96:97], amx[:, 0:24], AX.X, ALU.max)
                        vector.maybe_drain_then_inc((sV, 1))
                    C["sV"] += 1
                    ar_round(block, [amx[:, 96:97]])

                    @block.vector
                    def _(vector, need=C["sV"]):
                        vector.wait_ge(sV, need)
                        vector.tensor_tensor(sc[0:1, 1:2], sc[0:1, 48:49], sc[0:1, 0:1], ALU.mult)
                        vector.drain()
                        vector.tensor_scalar(sc[0:1, 1:2], sc[0:1, 1:2], QMAXI, 1e-12,
                                             ALU.mult, ALU.max)
                        vector.drain()
                        vector.reciprocal(sc[0:1, 2:3], sc[0:1, 1:2])
                        vector.drain()
                        vector.tensor_tensor(sc[0:1, 32:33], sc[0:1, 0:1], sc[0:1, 2:3], ALU.mult)
                        vector.tensor_copy(sc[0:1, 33:34], sc[0:1, 1:2])
                        vector.maybe_drain_then_inc((sV, 1))
                    C["sV"] += 1
                    bcast_scalars(block, 2, [1, 2])

                    @block.scalar
                    def _(scalar):
                        scalar.preload_act_table(AF.Sigmoid)

                    ffn_mm("up", wup_d, 24)

                    @block.vector
                    def _(vector, need=C["sV"]):
                        vector.wait_ge(sV, need)
                        vector.tensor_reduce(amx[:, 98:99], amx[:, 0:24], AX.X, ALU.max)
                        vector.drain()
                        vector.tensor_reduce(amx[:, 99:100], amx[:, 24:48], AX.X, ALU.max)
                        vector.maybe_drain_then_inc((sV, 1))
                    C["sV"] += 1
                    ar_round(block, [amx[:, 98:99], amx[:, 99:100]])

                    @block.vector
                    def _(vector, need=C["sV"]):
                        vector.wait_ge(sV, need)
                        vector.tensor_scalar(sc[0:1, 3:4], sc[0:1, 1:2], 1.0 / 65536.0, None, ALU.mult)
                        vector.drain()
                        vector.tensor_tensor(sc[0:1, 4:5], sc[0:1, 48:49], sc[0:1, 3:4], ALU.mult)
                        vector.drain()
                        vector.tensor_scalar(sc[0:1, 4:5], sc[0:1, 4:5], QMAXI, 1e-12, ALU.mult, ALU.max)
                        vector.drain()
                        vector.tensor_tensor(sc[0:1, 5:6], sc[0:1, 49:50], sc[0:1, 0:1], ALU.mult)
                        vector.drain()
                        vector.tensor_scalar(sc[0:1, 5:6], sc[0:1, 5:6], QMAXI, 1e-12, ALU.mult, ALU.max)
                        vector.drain()
                        vector.reciprocal(sc[0:1, 6:7], sc[0:1, 4:5])
                        vector.reciprocal(sc[0:1, 7:8], sc[0:1, 5:6])
                        vector.drain()
                        vector.tensor_tensor(sc[0:1, 32:33], sc[0:1, 3:4], sc[0:1, 6:7], ALU.mult)
                        vector.tensor_tensor(sc[0:1, 33:34], sc[0:1, 0:1], sc[0:1, 7:8], ALU.mult)
                        vector.maybe_drain_then_inc((sV, 1))
                    C["sV"] += 1
                    bcast_scalars(block, 2, [3, 4])

                    up_q_free = {0: 0, 1: 0}
                    for f in range(NF):
                        par = f % 2

                        @block.sync
                        def _(sync, f=f, par=par, free=up_q_free[par], bc=C["dS"]):
                            sync.wait_ge(dS, bc)
                            if free:
                                sync.wait_ge(sV, free)
                            sync.dma_start(out=upst[par][:, :], in_=upb_d[f, :, :]).then_inc(dS, 16)
                        C["dS"] += 16

                        @block.vector
                        def _(vector, f=f, par=par, need=C["dS"]):
                            vector.wait_ge(dS, need)
                            vector.tensor_scalar(scrA[:, :], gate[:, f, :], scb[:, 3:4], MAGIC,
                                                 ALU.mult, ALU.add)
                            vector.tensor_scalar(scrB[:, :], upst[par][:, :], scb[:, 4:5], MAGIC,
                                                 ALU.mult, ALU.add)
                            vector.drain()
                            vector.tensor_scalar(scrA[:, :], scrA[:, :], MAGIC, None, ALU.subtract)
                            vector.tensor_scalar(scrB[:, :], scrB[:, :], MAGIC, None, ALU.subtract)
                            vector.drain()
                            vector.tensor_tensor(gate[:, f, :], scrA[:, :], scrB[:, :], ALU.mult)
                            vector.drain()
                            vector.tensor_reduce(amx[:, f:f + 1], gate[:, f, :], AX.X, ALU.max,
                                                 apply_absolute_value=True)
                            vector.maybe_drain_then_inc((sV, 1))
                        C["sV"] += 1
                        up_q_free[par] = C["sV"]

                    @block.vector
                    def _(vector, need=C["sV"]):
                        vector.wait_ge(sV, need)
                        vector.tensor_reduce(amx[:, 100:101], amx[:, 0:24], AX.X, ALU.max)
                        vector.maybe_drain_then_inc((sV, 1))
                    C["sV"] += 1
                    ar_round(block, [amx[:, 100:101]])

                    @block.vector
                    def _(vector, need=C["sV"]):
                        vector.wait_ge(sV, need)
                        vector.tensor_tensor(sc[0:1, 8:9], sc[0:1, 4:5], sc[0:1, 5:6], ALU.mult)
                        vector.drain()
                        vector.tensor_tensor(sc[0:1, 9:10], sc[0:1, 48:49], sc[0:1, 8:9], ALU.mult)
                        vector.drain()
                        vector.tensor_scalar(sc[0:1, 9:10], sc[0:1, 9:10], QMAXI, 1e-12, ALU.mult, ALU.max)
                        vector.drain()
                        vector.reciprocal(sc[0:1, 10:11], sc[0:1, 9:10])
                        vector.drain()
                        vector.tensor_tensor(sc[0:1, 32:33], sc[0:1, 8:9], sc[0:1, 10:11], ALU.mult)
                        vector.tensor_copy(sc[0:1, 33:34], sc[0:1, 9:10])
                        vector.maybe_drain_then_inc((sV, 1))
                    C["sV"] += 1
                    bcast_scalars(block, 2, [5, 6])

                    for f in range(NF):
                        @block.vector
                        def _(vector, f=f, need=C["dS"]):
                            vector.wait_ge(dS, need)
                            vector.tensor_scalar(scrA[:, :], gate[:, f, :], scb[:, 5:6], MAGIC,
                                                 ALU.mult, ALU.add)
                            vector.drain()
                            vector.tensor_scalar(gate[:, f, :], scrA[:, :], MAGIC, None, ALU.subtract)
                            vector.maybe_drain_then_inc((sV, 1))
                        C["sV"] += 1

                        @block.sync
                        def _(sync, f=f, need=C["sV"]):
                            sync.wait_ge(sV, need)
                            sync.dma_start(out=upb_d[f, :, :], in_=gate[:, f, :]).then_inc(dS, 16)
                        C["dS"] += 16
                    oint_dS = C["dS"]

            with contextlib.ExitStack() as s3c:
                wdT = s3c.enter_context(nc.sbuf_tensor("wdT", [128, NF, 1024], BF16))
                ob = [s3c.enter_context(nc.sbuf_tensor(f"ob{i}", [128, NF, 128], BF16)) for i in range(2)]
                wst3 = [s3c.enter_context(nc.sbuf_tensor(f"wst3{i}", [128, I3], F32)) for i in range(2)]
                wdg = [s3c.enter_context(nc.sbuf_tensor(f"wdg{i}", [128, I3], BF16)) for i in range(2)]
                sq3 = s3c.enter_context(nc.sbuf_tensor("sq3", [128, 96], F32))
                sqi3 = s3c.enter_context(nc.sbuf_tensor("sqi3", [128, 96], F32))
                xr3 = s3c.enter_context(nc.sbuf_tensor("xr3", [128, 2, 1024], F32))

                with nc.Block() as block:
                    wst3_free = {0: 0, 1: 0}
                    wdg_free = {0: 0, 1: 0}
                    tposf = {0: 0, 1: 0}
                    for m in range(8):
                        par3 = m % 2

                        @block.sync
                        def _(sync, m=m, par3=par3, free=wst3_free[par3]):
                            if free:
                                sync.wait_ge(sV, free)
                            sync.dma_start(out=wst3[par3][:, :],
                                           in_=wdn_d[m * 128:(m + 1) * 128, :]).then_inc(dS, 16)
                        C["dS"] += 16

                        @block.vector
                        def _(vector, m=m, par3=par3, need=C["dS"], tneed=wdg_free[par3]):
                            vector.wait_ge(dS, need)
                            if tneed:
                                vector.wait_ge(sT, tneed)
                            w3 = wst3[par3][:, :].rearrange("p (b c) -> p b c", c=32)
                            a3 = scrA[:, 0:2048].rearrange("p (b c) -> p b c", c=32)
                            a3b = scrB[:, 0:1024].rearrange("p (b c) -> p b c", c=32)
                            vector.tensor_reduce(sq3[:, :].rearrange("p (b o) -> p b o", o=1),
                                                 w3, AX.X, ALU.max, apply_absolute_value=True)
                            vector.drain()
                            vector.tensor_scalar(sq3[:, :], sq3[:, :], 1.0 / 7.0, 1e-12,
                                                 ALU.mult, ALU.max)
                            vector.drain()
                            vector.reciprocal(sqi3[:, :], sq3[:, :])
                            vector.drain()
                            vector.tensor_tensor(a3, w3[:, 0:64, :],
                                                 sqi3[:, 0:64].rearrange("p (b o) -> p b o", o=1)
                                                 .broadcast_to([128, 64, 32]), ALU.mult)
                            vector.tensor_tensor(a3b, w3[:, 64:96, :],
                                                 sqi3[:, 64:96].rearrange("p (b o) -> p b o", o=1)
                                                 .broadcast_to([128, 32, 32]), ALU.mult)
                            vector.drain()
                            vector.tensor_scalar(scrA[:, 0:2048], scrA[:, 0:2048], MAGIC, MAGIC,
                                                 ALU.add, ALU.subtract)
                            vector.tensor_scalar(scrB[:, 0:1024], scrB[:, 0:1024], MAGIC, MAGIC,
                                                 ALU.add, ALU.subtract)
                            vector.drain()
                            vector.tensor_tensor(wdg[par3][:, 0:2048].rearrange("p (b c) -> p b c", c=32),
                                                 a3, sq3[:, 0:64].rearrange("p (b o) -> p b o", o=1)
                                                 .broadcast_to([128, 64, 32]), ALU.mult)
                            vector.tensor_tensor(wdg[par3][:, 2048:3072].rearrange("p (b c) -> p b c", c=32),
                                                 a3b, sq3[:, 64:96].rearrange("p (b o) -> p b o", o=1)
                                                 .broadcast_to([128, 32, 32]), ALU.mult)
                            vector.maybe_drain_then_inc((sV, 1))
                        C["sV"] += 1
                        wst3_free[par3] = C["sV"]

                        for fg in range(3):
                            par = fg % 2

                            @block.tensor
                            def _(tensor, m=m, fg=fg, par=par, par3=par3, need=C["sV"],
                                  pfree=tposf[par]):
                                tensor.wait_ge(sV, max(need, pfree))
                                for c in range(8):
                                    f = fg * 8 + c
                                    ins = tensor.transpose(ptp3[par][:, c * 128:(c + 1) * 128],
                                                           wdg[par3][:, f * 128:(f + 1) * 128], ident[:, :])
                                ins.then_inc(sT, 1)
                            C["sT"] += 1
                            if fg == 2:
                                wdg_free[par3] = C["sT"]

                            @block.vector
                            def _(vector, m=m, fg=fg, par=par, need=C["sT"]):
                                vector.wait_ge(sT, need)
                                vector.tensor_copy(
                                    wdT[:, fg * 8:(fg + 1) * 8, m * 128:(m + 1) * 128],
                                    ptp3[par][:, :].rearrange("p (c q) -> p c q", c=8))
                                vector.maybe_drain_then_inc((sV, 1))
                            C["sV"] += 1
                            tposf[par] = C["sV"]

                    xr_free = {0: 0, 1: 0}
                    pgf_free = {0: 0, 1: 0}
                    ob_free = {0: 0, 1: 0}
                    for t in range(16):
                        par = t % 2

                        @block.sync
                        def _(sync, t=t, par=par,
                              free=(C["dS"] if (xr_free[par] or t < 2) else 0),
                              obfree=ob_free[par]):
                            if free:
                                sync.wait_ge(dS, free)
                            if obfree:
                                sync.wait_ge(sT, obfree)
                            sync.dma_start(out=xr3[:, par, :],
                                           in_=x2_d[t * 128:(t + 1) * 128, :]).then_inc(dS, 16)
                            sync.dma_start(
                                out=ob[par][:, :, :],
                                in_=upb_d[:, :, t * 128:(t + 1) * 128].rearrange("f p t -> p f t"),
                            ).then_inc(dS, 16)
                        C["dS"] += 32
                        xr_dma = C["dS"]

                        for half in range(2):
                            @block.tensor
                            def _(tensor, t=t, par=par, half=half, need=max(tposf.values()),
                                  pfree=pgf_free[half], obd=xr_dma):
                                tensor.wait_ge(sV, max(need, pfree))
                                tensor.wait_ge(dS, obd)
                                for f in range(NF):
                                    ins = tensor.matmul(pg3[half][:, :],
                                                        ob[par][:, f, :],
                                                        wdT[:, f, half * 512:(half + 1) * 512],
                                                        start=(f == 0), stop=(f == NF - 1))
                                ins.then_inc(sT, 1)
                            C["sT"] += 1
                            ob_free[par] = C["sT"]

                            @block.vector
                            def _(vector, t=t, par=par, half=half, need=C["sT"], xd=xr_dma):
                                vector.wait_ge(sT, need)
                                vector.wait_ge(dS, xd)
                                vector.scalar_tensor_tensor(
                                    xr3[:, par, half * 512:(half + 1) * 512],
                                    pg3[half][:, :], scb[:, 6:7],
                                    xr3[:, par, half * 512:(half + 1) * 512],
                                    ALU.mult, ALU.add)
                                vector.maybe_drain_then_inc((sV, 1))
                            C["sV"] += 1
                            pgf_free[half] = C["sV"]

                        @block.sync
                        def _(sync, t=t, par=par, need=C["sV"]):
                            sync.wait_ge(sV, need)
                            sync.dma_start(out=out_d[t * 128:(t + 1) * 128, :],
                                           in_=xr3[:, par, :]).then_inc(dS, 16)
                        C["dS"] += 16
                        xr_free[par] = C["dS"]

                    @block.sync
                    def _(sync, need=C["dS"]):
                        sync.wait_ge(dS, need)

    return nc


_NC_CACHE = {}


def kernel(L0_attn_x, L0_attn_Q_w, L0_attn_Q_b, L0_attn_K_w, L0_attn_K_b,
           L0_attn_V_w, L0_attn_V_b, L0_attn_O_w, L0_attn_O_b,
           L0_ffn_w_gate, L0_ffn_w_up, L0_ffn_w_down):
    x = np.asarray(L0_attn_x, np.float32)
    bf = ml_dtypes.bfloat16
    wq = np.ascontiguousarray(np.asarray(L0_attn_Q_w, np.float32).transpose(1, 0, 2).reshape(D, D))
    wk = np.ascontiguousarray(np.asarray(L0_attn_K_w, np.float32).transpose(1, 0, 2).reshape(D, D))
    wv = np.ascontiguousarray(np.asarray(L0_attn_V_w, np.float32).transpose(1, 0, 2).reshape(D, D))
    wo = np.ascontiguousarray(np.asarray(L0_attn_O_w, np.float32).reshape(D, D))
    qbv = np.ascontiguousarray(np.asarray(L0_attn_Q_b, np.float32).reshape(8, 128).T)
    kbv = np.ascontiguousarray(np.asarray(L0_attn_K_b, np.float32).reshape(8, 128).T)
    vbv = np.ascontiguousarray(np.asarray(L0_attn_V_b, np.float32).reshape(1, D))
    obv = np.ascontiguousarray(np.asarray(L0_attn_O_b, np.float32).reshape(1, D))
    wup = np.ascontiguousarray(np.asarray(L0_ffn_w_up, np.float32))
    wgt = np.ascontiguousarray(np.asarray(L0_ffn_w_gate, np.float32))
    wdn = np.ascontiguousarray(np.asarray(L0_ffn_w_down, np.float32))
    ident = np.eye(128, dtype=np.float32).astype(bf)
    selm = np.zeros((128, 256), np.float32)
    selm[64, 0:128] = 1.0
    selm[63, 128:256] = 1.0

    if "nc" not in _NC_CACHE:
        _NC_CACHE["nc"] = build_nc()
    nc = _NC_CACHE["nc"]

    in_maps = []
    row_idx = []
    for c in range(8):
        b, p = c // 2, c % 2
        rows = np.concatenate([np.arange(512 * j + 256 * p, 512 * j + 256 * p + 256)
                               for j in range(NJ)])
        row_idx.append((b, rows))
        tt = np.arange(4)[:, None, None]
        kk = np.arange(128)[None, :, None]
        qq = np.arange(256)[None, None, :]
        keep = (128 * tt + kk) <= (256 * p + qq)
        msk = keep.astype(np.float32).astype(bf)
        in_maps.append({
            "xq": np.ascontiguousarray(x[b][rows]),
            "xkv": np.ascontiguousarray(x[b]),
            "wq": wq, "wk": wk, "wv": wv, "wo": wo,
            "qbias": qbv, "kbias": kbv, "vbias": vbv, "obias": obv,
            "masks": msk, "ident": ident, "selm": selm,
            "wup": wup, "wgt": wgt, "wdn": wdn,
        })

    res = run_bass_kernel_spmd(nc, in_maps, core_ids=list(range(8)))
    out = np.empty((B, S, D), np.float32)
    for c in range(8):
        b, rows = row_idx[c]
        out[b][rows] = np.asarray(res.results[c]["out"], np.float32)
    return out


# version 19
# speedup vs baseline: 1.0119x; 1.0119x over previous
import contextlib
import numpy as np
import ml_dtypes
import concourse.bass as bass
import concourse.mybir as mybir
from concourse.bass_utils import run_bass_kernel_spmd

F32 = mybir.dt.float32
BF16 = mybir.dt.bfloat16
AF = mybir.ActivationFunctionType
ALU = mybir.AluOpType
AX = mybir.AxisListType

MAGIC = 12582912.0
QMAXI = 1.0 / 32767.0
B, S, D = 4, 4096, 1024
NP = 8
NJ = 8
NTOK = 2048
I3, NF = 3072, 24
EPS = 1e-5


def build_nc(attn_only=False, stage=3):
    if attn_only:
        stage = 2
    nc = bass.Bass("TRN2", target_bir_lowering=False)
    xq_d = nc.dram_tensor("xq", [NTOK, D], F32, kind="ExternalInput").ap()
    xkv_d = nc.dram_tensor("xkv", [S, D], F32, kind="ExternalInput").ap()
    wq_d = nc.dram_tensor("wq", [D, D], F32, kind="ExternalInput").ap()
    wk_d = nc.dram_tensor("wk", [D, D], F32, kind="ExternalInput").ap()
    wv_d = nc.dram_tensor("wv", [D, D], F32, kind="ExternalInput").ap()
    wo_d = nc.dram_tensor("wo", [D, D], F32, kind="ExternalInput").ap()
    qb_d = nc.dram_tensor("qbias", [128, NP], F32, kind="ExternalInput").ap()
    kb_d = nc.dram_tensor("kbias", [128, NP], F32, kind="ExternalInput").ap()
    vb_d = nc.dram_tensor("vbias", [1, D], F32, kind="ExternalInput").ap()
    ob_d = nc.dram_tensor("obias", [1, D], F32, kind="ExternalInput").ap()
    msk_d = nc.dram_tensor("masks", [4, 128, 256], BF16, kind="ExternalInput").ap()
    id_d = nc.dram_tensor("ident", [128, 128], BF16, kind="ExternalInput").ap()
    sel_d = nc.dram_tensor("selm", [128, 256], F32, kind="ExternalInput").ap()
    wup_d = nc.dram_tensor("wup", [I3, D], F32, kind="ExternalInput").ap()
    wgt_d = nc.dram_tensor("wgt", [I3, D], F32, kind="ExternalInput").ap()
    wdn_d = nc.dram_tensor("wdn", [D, I3], F32, kind="ExternalInput").ap()
    out_d = nc.dram_tensor("out", [NTOK, D], F32, kind="ExternalOutput").ap()
    qbuf_d = nc.dram_tensor("qbuf", [128, NP, NTOK], BF16).ap()
    x2_d = nc.dram_tensor("x2buf", [NTOK, D], F32).ap()
    upb_d = nc.dram_tensor("upbuf", [NF, 128, NTOK], BF16).ap()
    cci_d = nc.dram_tensor("cc_in", [2, 128], F32).ap()
    cco_d = nc.dram_tensor("cc_out", [2, 128], F32).ap()
    scal_d = nc.dram_tensor("scal", [1, 16], F32).ap()

    st = contextlib.ExitStack()
    with st:
        SB = lambda n, sh, dt: st.enter_context(nc.sbuf_tensor(n, sh, dt))
        ident = SB("identb", [128, 128], BF16)
        selm = SB("selmb", [128, 256], F32)
        ones1 = SB("ones1", [1, 128], F32)
        qbias = SB("qbiasb", [128, NP], F32)
        kbias = SB("kbiasb", [128, NP], F32)
        vbrow = SB("vbrow", [1, D], F32)
        obrow = SB("obrow", [1, D], F32)
        e5 = SB("e5", [128, 1], F32)
        sc = SB("scw", [1, 512], F32)
        scb = SB("scb", [128, 8], F32)
        amx = SB("amx", [128, 104], F32)

        dS = st.enter_context(nc.semaphore("dS"))
        dG = st.enter_context(nc.semaphore("dG"))
        sV = st.enter_context(nc.semaphore("sV"))
        sS = st.enter_context(nc.semaphore("sS"))
        sT = st.enter_context(nc.semaphore("sT"))
        sC = st.enter_context(nc.semaphore("sC"))
        dX = st.enter_context(nc.semaphore("dX"))
        SEMS = {"dS": dS, "dG": dG, "sV": sV, "sS": sS, "sT": sT, "sC": sC, "dX": dX}
        C = {k: 0 for k in SEMS}

        def W(eng, pairs):
            for name, cnt in pairs:
                if cnt:
                    eng.wait_ge(SEMS[name], cnt)

        with nc.Block() as b0:
            @b0.gpsimd
            def _(g):
                for s in SEMS.values():
                    g.sem_clear(s)

        satt = contextlib.ExitStack()
        kdat = satt.enter_context(nc.sbuf_tensor("kdat", [128, NP, S], BF16))
        vdat = satt.enter_context(nc.sbuf_tensor("vdat", [128, 32, NP * 130], BF16))
        masks = satt.enter_context(nc.sbuf_tensor("masksb", [128, 4, 256], BF16))

        with contextlib.ExitStack() as s1:
            B1 = lambda n, sh, dt: s1.enter_context(nc.sbuf_tensor(n, sh, dt))
            xst = B1("xst", [128, 2, 1024], F32)
            xln = B1("xln", [128, 2, 1024], BF16)
            xfm = B1("xfm", [128, 8, 256], BF16)
            bns = B1("bns", [128, 24], F32)
            mv = B1("mv", [128, 8], F32)
            wA = B1("wA", [128, 8, 1024], BF16)
            wB = B1("wB", [128, 8, 1024], BF16)
            qst = B1("qst", [128, 8, 256], BF16)
            ptp = [s1.enter_context(nc.psum_tensor(f"ptp{i}", [128, 1024], BF16)) for i in range(2)]
            pk = [s1.enter_context(nc.psum_tensor(f"pk{i}", [128, 512], F32)) for i in range(2)]

            with nc.Block() as block:
                @block.sync
                def _(sync):
                    sync.dma_start(out=masks[:, :, :], in_=msk_d.rearrange("t p q -> p t q")).then_inc(dS, 16)
                    sync.dma_start(out=ident[:, :], in_=id_d).then_inc(dS, 16)
                    sync.dma_start(out=selm[:, :], in_=sel_d).then_inc(dS, 16)
                    sync.dma_start(out=qbias[:, :], in_=qb_d).then_inc(dS, 16)
                    sync.dma_start(out=kbias[:, :], in_=kb_d).then_inc(dS, 16)
                    sync.dma_start(out=vbrow[:, :], in_=vb_d).then_inc(dS, 16)
                    sync.dma_start(out=obrow[:, :], in_=ob_d).then_inc(dS, 16)
                C["dS"] += 7 * 16

                @block.gpsimd
                def _(gpsimd):
                    gpsimd.dma_start(out=wA[:, :, :],
                                     in_=wk_d.rearrange("(c p) m -> p c m", p=128)).then_inc(dG, 16)
                    gpsimd.dma_start(out=wB[:, :, :],
                                     in_=wv_d.rearrange("(c p) m -> p c m", p=128)).then_inc(dG, 16)
                C["dG"] += 32
                kv_w_dG = C["dG"]

                @block.scalar
                def _(scalar):
                    scalar.preload_act_table(AF.Sqrt)

                @block.vector
                def _(vector):
                    vector.memset(ones1[:, :], 1.0)
                    vector.memset(e5[:, :], EPS)
                    vector.memset(
                        vdat[:, :, :].rearrange("p t (g c) -> p t g c", c=130)[:, :, :, 64:66], 1.0)
                    vector.maybe_drain_then_inc((sV, 1))
                C["sV"] += 1

                xst_free = 0
                pkfree = {0: ("sS", 0), 1: ("sS", 0)}
                ptp_free = {0: 0, 1: 0}
                qst_dma = 0
                wq_dG = None

                for sl in range(24):
                    kind = "kv" if sl < 16 else "q"
                    sidx = sl if sl < 16 else sl - 16

                    if kind == "q" and sidx == 0:
                        t_gate = C["sT"]

                        @block.gpsimd
                        def _(gpsimd, t_gate=t_gate):
                            gpsimd.wait_ge(sT, t_gate)
                            gpsimd.dma_start(out=wA[:, :, :],
                                             in_=wq_d.rearrange("(c p) m -> p c m", p=128)).then_inc(dG, 16)
                        C["dG"] += 16
                        wq_dG = C["dG"]

                    @block.sync
                    def _(sync, kind=kind, sidx=sidx, need_v=xst_free):
                        if need_v:
                            sync.wait_ge(sV, need_v)
                        src = xkv_d if kind == "kv" else xq_d
                        sync.dma_start(
                            out=xst[:, :, :],
                            in_=src[sidx * 256:(sidx + 1) * 256, :].rearrange("(t p) d -> p t d", p=128),
                        ).then_inc(dS, 16)
                    C["dS"] += 16
                    x_dma = C["dS"]

                    @block.vector
                    def _(vector, x_dma=x_dma):
                        vector.wait_ge(dS, x_dma)
                        vector.bn_stats(bns[:, 0:6], xst[:, 0, 0:512])
                        vector.bn_stats(bns[:, 6:12], xst[:, 0, 512:1024])
                        vector.bn_stats(bns[:, 12:18], xst[:, 1, 0:512])
                        vector.bn_stats(bns[:, 18:24], xst[:, 1, 512:1024])
                        vector.drain()
                        vector.bn_aggr(mv[:, 0:2], bns[:, 0:12])
                        vector.bn_aggr(mv[:, 2:4], bns[:, 12:24])
                        vector.drain()
                        vector.tensor_scalar_add(mv[:, 4:5], mv[:, 1:2], EPS)
                        vector.tensor_scalar_add(mv[:, 5:6], mv[:, 3:4], EPS)
                        vector.maybe_drain_then_inc((sV, 1))
                    C["sV"] += 1

                    @block.scalar
                    def _(scalar, need=C["sV"]):
                        scalar.wait_ge(sV, need)
                        scalar.sqrt(mv[:, 6:8], mv[:, 4:6])
                        scalar.maybe_drain_then_inc((sS, 1))
                    C["sS"] += 1

                    @block.vector
                    def _(vector, need=C["sS"]):
                        vector.wait_ge(sS, need)
                        vector.reciprocal(mv[:, 6:8], mv[:, 6:8])
                        vector.drain()
                        vector.tensor_scalar(xln[:, 0, :], xst[:, 0, :], mv[:, 0:1], mv[:, 6:7],
                                             ALU.subtract, ALU.mult)
                        vector.tensor_scalar(xln[:, 1, :], xst[:, 1, :], mv[:, 2:3], mv[:, 7:8],
                                             ALU.subtract, ALU.mult)
                        vector.maybe_drain_then_inc((sV, 1))
                    C["sV"] += 1
                    xst_free = C["sV"]
                    norm_done = C["sV"]

                    for cg in range(2):
                        par = cg

                        @block.tensor
                        def _(tensor, cg=cg, par=par, need_v=norm_done, pfree=ptp_free[par]):
                            tensor.wait_ge(sV, max(need_v, pfree))
                            for cl in range(4):
                                c = 4 * cg + cl
                                for t in range(2):
                                    ins = tensor.transpose(
                                        ptp[par][:, (cl * 2 + t) * 128:(cl * 2 + t + 1) * 128],
                                        xln[:, t, c * 128:(c + 1) * 128], ident[:, :])
                            ins.then_inc(sT, 1)
                        C["sT"] += 1

                        @block.vector
                        def _(vector, cg=cg, par=par, need=C["sT"]):
                            vector.wait_ge(sT, need)
                            vector.tensor_copy(
                                xfm[:, 4 * cg:4 * cg + 4, :],
                                ptp[par][:, :].rearrange("p (cl q) -> p cl q", cl=4))
                            vector.maybe_drain_then_inc((sV, 1))
                        C["sV"] += 1
                        ptp_free[par] = C["sV"]
                    xfm_done = C["sV"]

                    if kind == "kv":
                        for g in range(NP):
                            par = g % 2
                            fsem, fcnt = pkfree[par]

                            @block.tensor
                            def _(tensor, g=g, par=par, need_v=xfm_done, fsem=fsem, fcnt=fcnt,
                                  first=(sl == 0 and g == 0), kv_w_dG=kv_w_dG):
                                tensor.wait_ge(sV, need_v)
                                W(tensor, [(fsem, fcnt)])
                                if first:
                                    tensor.wait_ge(dG, kv_w_dG)
                                for c in range(8):
                                    ins = tensor.matmul(pk[par][:, 0:256],
                                                        wA[:, c, g * 128:(g + 1) * 128],
                                                        xfm[:, c, :], start=(c == 0), stop=(c == 7))
                                ins.then_inc(sT, 1)
                            C["sT"] += 1

                            @block.vector
                            def _(vector, g=g, par=par, sidx=sidx, need=C["sT"]):
                                vector.wait_ge(sT, need)
                                vector.tensor_scalar(kdat[:, g, sidx * 256:(sidx + 1) * 256],
                                                     pk[par][:, 0:256], kbias[:, g:g + 1], None,
                                                     ALU.add)
                                vector.maybe_drain_then_inc((sV, 1))
                            C["sV"] += 1
                            pkfree[par] = ("sV", C["sV"])

                        for tt in range(2):
                            for half in range(2):
                                par = (tt * 2 + half) % 2
                                fsem, fcnt = pkfree[par]

                                @block.tensor
                                def _(tensor, tt=tt, half=half, par=par, need_v=xfm_done,
                                      fsem=fsem, fcnt=fcnt):
                                    tensor.wait_ge(sV, need_v)
                                    W(tensor, [(fsem, fcnt)])
                                    for c in range(8):
                                        tensor.matmul(pk[par][:, :], xfm[:, c, tt * 128:(tt + 1) * 128],
                                                      wB[:, c, half * 512:(half + 1) * 512],
                                                      start=(c == 0), stop=False)
                                    tensor.matmul(pk[par][:, :], ones1[0:1, :],
                                                  vbrow[0:1, half * 512:(half + 1) * 512],
                                                  start=False, stop=True).then_inc(sT, 1)
                                C["sT"] += 1

                                @block.vector
                                def _(vector, tt=tt, half=half, par=par, sidx=sidx, need=C["sT"]):
                                    vector.wait_ge(sT, need)
                                    T = sidx * 2 + tt
                                    pv = pk[par][:, :].rearrange("p (g two c) -> p g two c", two=2, c=64)
                                    ov = vdat[:, T:T + 1, :].rearrange(
                                        "p one (g c) -> p (one g) c", c=130)[:, half * 4:(half + 1) * 4, :]
                                    vector.tensor_copy(ov[:, :, 0:64], pv[:, :, 0, :])
                                    vector.tensor_copy(ov[:, :, 66:130], pv[:, :, 1, :])
                                    vector.maybe_drain_then_inc((sV, 1))
                                C["sV"] += 1
                                pkfree[par] = ("sV", C["sV"])
                    else:
                        for g in range(NP):
                            par = g % 2
                            fsem, fcnt = pkfree[par]

                            @block.tensor
                            def _(tensor, g=g, par=par, need_v=xfm_done, fsem=fsem, fcnt=fcnt,
                                  first=(sidx == 0 and g == 0), wq_dG=wq_dG):
                                tensor.wait_ge(sV, need_v)
                                W(tensor, [(fsem, fcnt)])
                                if first:
                                    tensor.wait_ge(dG, wq_dG)
                                for c in range(8):
                                    ins = tensor.matmul(pk[par][:, 0:256],
                                                        wA[:, c, g * 128:(g + 1) * 128],
                                                        xfm[:, c, :], start=(c == 0), stop=(c == 7))
                                ins.then_inc(sT, 1)
                            C["sT"] += 1

                            @block.vector
                            def _(vector, g=g, par=par, need=C["sT"], qfree=(C["dS"] if (qst_dma and g == 0) else 0)):
                                vector.wait_ge(sT, need)
                                if qfree:
                                    vector.wait_ge(dS, qfree)
                                vector.tensor_scalar(qst[:, g, :], pk[par][:, 0:256],
                                                     qbias[:, g:g + 1], None, ALU.add)
                                vector.maybe_drain_then_inc((sV, 1))
                            C["sV"] += 1
                            pkfree[par] = ("sV", C["sV"])

                        @block.sync
                        def _(sync, sidx=sidx, need=C["sV"]):
                            sync.wait_ge(sV, need)
                            sync.dma_start(out=qbuf_d[:, :, sidx * 256:(sidx + 1) * 256],
                                           in_=qst[:, :, :]).then_inc(dS, 16)
                        C["dS"] += 16
                        qst_dma = C["dS"]

            p1_dS = C["dS"]

        if stage == 1:
            with nc.Block() as block:
                @block.gpsimd
                def _(g, sSn=C["sS"], sVn=C["sV"], dSn=C["dS"]):
                    g.wait_ge(sS, sSn)
                    g.wait_ge(sV, sVn)
                    g.wait_ge(dS, dSn)
                    g.dma_start(out=out_d[0:128, :], in_=kdat[:, 0, 0:1024]).then_inc(dG, 16)
                    g.dma_start(out=out_d[128:256, :], in_=kdat[:, 7, 3072:4096]).then_inc(dG, 16)
                    g.dma_start(out=out_d[256:384, :], in_=qbuf_d[:, 0, 0:1024]).then_inc(dG, 16)
                    g.dma_start(out=out_d[384:512, :], in_=qbuf_d[:, 7, 1024:2048]).then_inc(dG, 16)
                    g.dma_start(out=out_d[512:640, :], in_=vdat[:, 0, 0:1024]).then_inc(dG, 16)
                    g.dma_start(out=out_d[640:768, :], in_=vdat[:, 31, 0:1024]).then_inc(dG, 16)
                    g.wait_ge(dG, C["dG"] + 96)
            satt.close()
            return nc

        with contextlib.ExitStack() as s2:
            B2 = lambda n, sh, dt: s2.enter_context(nc.sbuf_tensor(n, sh, dt))
            woS = B2("woS", [128, 8, 1024], BF16)
            qj = [B2(f"qj{i}", [128, NP, 256], BF16) for i in range(2)]
            Pb = [B2(f"Pb{i}", [128, 2048], BF16) for i in range(2)]
            attn = B2("attn", [128, 8, 256], BF16)
            lrow = B2("lrow", [128, 256], F32)
            lbc = B2("lbc", [128, 256], F32)
            lbc2 = B2("lbc2", [128, 256], F32)
            xres = B2("xres", [128, 2, 1024], F32)
            ps_s = s2.enter_context(nc.psum_tensor("pss", [128, 2048], F32))
            EO = [s2.enter_context(nc.psum_tensor(f"EO{i}", [128, 512], F32)) for i in range(2)]
            pbb = s2.enter_context(nc.psum_tensor("pbb", [128, 512], F32))
            py = s2.enter_context(nc.psum_tensor("py", [128, 512], F32))

            with nc.Block() as block:
                @block.gpsimd
                def _(gpsimd):
                    gpsimd.dma_start(out=woS[:, :, :],
                                     in_=wo_d.rearrange("(c p) m -> p c m", p=128)).then_inc(dG, 16)
                C["dG"] += 16
                wo_dG = C["dG"]

                @block.scalar
                def _(scalar):
                    scalar.preload_act_table(AF.Exp)

                @block.vector
                def _(vector):
                    vector.memset(lrow[:, :], 0.0)
                    vector.memset(EO[0][64:96, 0:256], 1.0)
                    vector.memset(EO[1][64:96, 0:256], 1.0)
                    vector.maybe_drain_then_inc((sV, 1))
                C["sV"] += 1

                exp_hist = 0
                pv_hist = {0: 0, 1: 0}
                evac_hist = {0: C["sV"], 1: C["sV"]}
                lrow_hist = 0
                lbc_hist = 0
                oproj_hist = 0
                py_free = 0
                xres_dma = 0
                gseq = 0

                for j in range(NJ):
                    qj_cur = qj[j % 2]

                    @block.sync
                    def _(sync, j=j, need=C["dS"], xfree=py_free):
                        sync.wait_ge(dS, need)
                        if xfree:
                            sync.wait_ge(sV, xfree)
                        sync.dma_start(out=qj_cur[:, :, :],
                                       in_=qbuf_d[:, :, j * 256:(j + 1) * 256]).then_inc(dS, 16)
                        sync.dma_start(
                            out=xres[:, :, :],
                            in_=xq_d[j * 256:(j + 1) * 256, :].rearrange("(t p) d -> p t d", p=128),
                        ).then_inc(dX, 16)
                    C["dS"] += 16
                    C["dX"] += 16
                    qj_ready = C["dS"]
                    xres_dma = C["dX"]

                    for g in range(NP):
                        ngrp2 = j + 1
                        gp = gseq % 2

                        def emit_pv(kg2, par2, p_gate, gp, ngrp2, g):
                            @block.tensor
                            def _(tensor, kg2=kg2, par2=par2, p_gate=p_gate, gp=gp,
                                  ngrp2=ngrp2, g=g,
                                  ev_free=(evac_hist[gp] if kg2 == 0 else 0)):
                                W(tensor, [p_gate])
                                if ev_free:
                                    tensor.wait_ge(sV, ev_free)
                                for kt in range(4):
                                    T = 4 * kg2 + kt
                                    first = (kg2 == 0 and kt == 0)
                                    last = (kg2 == ngrp2 - 1 and kt == 3)
                                    mmO = lambda s, p: tensor.matmul(
                                        EO[gp][0:128, 256:512],
                                        vdat[:, T, g * 130 + 2:g * 130 + 130],
                                        Pb[par2][:, 1024 + kt * 256:1024 + (kt + 1) * 256],
                                        start=s, stop=p)
                                    mmE = lambda: tensor.matmul(
                                        EO[gp][0:65, 0:256],
                                        vdat[:, T, g * 130:g * 130 + 65],
                                        Pb[par2][:, kt * 256:(kt + 1) * 256],
                                        start=False, stop=False)
                                    if last:
                                        mmE()
                                        ins = mmO(False, True)
                                    else:
                                        mmO(first, False)
                                        ins = mmE()
                                ins.then_inc(sT, 1)
                            C["sT"] += 1
                            pv_hist[par2] = C["sT"]

                        pend = None
                        for kg2 in range(ngrp2):
                            par2 = kg2 % 2

                            @block.tensor
                            def _(tensor, j=j, g=g, kg2=kg2, qj_cur=qj_cur, need_q=qj_ready,
                                  need_wo=(wo_dG if (j == 0 and g == 0 and kg2 == 0) else 0),
                                  exp_free=exp_hist):
                                tensor.wait_ge(dS, need_q)
                                if need_wo:
                                    tensor.wait_ge(dG, need_wo)
                                if exp_free:
                                    tensor.wait_ge(sS, exp_free)
                                for kt in range(4):
                                    T = 4 * kg2 + kt
                                    for hh in range(2):
                                        ins = tensor.matmul(
                                            ps_s[:, hh * 1024 + kt * 256:hh * 1024 + (kt + 1) * 256],
                                            kdat[hh * 64:(hh + 1) * 64, g, T * 128:(T + 1) * 128],
                                            qj_cur[hh * 64:(hh + 1) * 64, g, :],
                                            start=True, stop=True)
                                ins.then_inc(sT, 1)
                            C["sT"] += 1

                            @block.scalar
                            def _(scalar, par2=par2, need=C["sT"], pv_free=pv_hist[par2]):
                                scalar.wait_ge(sT, max(need, pv_free))
                                scalar.activation(Pb[par2][:, 0:1024], ps_s[:, 0:1024], AF.Exp,
                                                  bias=0.0, scale=0.125)
                                scalar.activation(Pb[par2][:, 1024:2048], ps_s[:, 1024:2048], AF.Exp,
                                                  bias=0.0, scale=0.125)
                                scalar.maybe_drain_then_inc((sS, 1))
                            C["sS"] += 1
                            exp_hist = C["sS"]

                            if kg2 == j:
                                @block.vector
                                def _(vector, par2=par2, need=C["sS"]):
                                    vector.wait_ge(sS, need)
                                    for hh in range(2):
                                        pv3 = Pb[par2][:, hh * 1024:(hh + 1) * 1024].rearrange(
                                            "p (t q) -> p t q", q=256)
                                        vector.tensor_tensor(pv3, pv3, masks[:, :, :], ALU.mult)
                                    vector.maybe_drain_then_inc((sV, 1))
                                C["sV"] += 1
                                p_gate = ("sV", C["sV"])
                            else:
                                p_gate = ("sS", C["sS"])

                            if pend is not None:
                                emit_pv(*pend)
                            pend = (kg2, par2, p_gate, gp, ngrp2, g)
                        emit_pv(*pend)

                        @block.vector
                        def _(vector, gp=gp, need=C["sT"], lfree=lrow_hist):
                            vector.wait_ge(sT, max(need, lfree))
                            vector.tensor_copy(lrow[64:96, :], EO[gp][64:96, 0:256])
                            vector.tensor_copy(lrow[32:64, :], EO[gp][32:64, 256:512])
                            vector.drain()
                            vector.tensor_scalar_max(lrow[32:64, :], lrow[32:64, :], 1e-35)
                            vector.tensor_scalar_max(lrow[64:96, :], lrow[64:96, :], 1e-35)
                            vector.drain()
                            vector.reciprocal(lrow[32:64, :], lrow[32:64, :])
                            vector.reciprocal(lrow[64:96, :], lrow[64:96, :])
                            vector.maybe_drain_then_inc((sV, 1))
                        C["sV"] += 1

                        @block.tensor
                        def _(tensor, need=C["sV"], lfree=lbc_hist):
                            tensor.wait_ge(sV, max(need, lfree))
                            tensor.matmul(pbb[:, 0:256], selm[:, 0:128], lrow[:, :],
                                          start=True, stop=True)
                            tensor.matmul(pbb[:, 256:512], selm[:, 128:256], lrow[:, :],
                                          start=True, stop=True).then_inc(sT, 1)
                        C["sT"] += 1
                        lrow_hist = C["sT"]

                        @block.vector
                        def _(vector, g=g, gp=gp, need=C["sT"], op_gate=oproj_hist):
                            vector.wait_ge(sT, max(need, op_gate))
                            vector.tensor_copy(lbc[:, :], pbb[:, 0:256])
                            vector.tensor_copy(lbc2[:, :], pbb[:, 256:512])
                            vector.drain()
                            vector.tensor_tensor(attn[0:64, g, :], EO[gp][0:64, 0:256],
                                                 lbc[0:64, :], ALU.mult)
                            vector.tensor_tensor(attn[64:128, g, :], EO[gp][64:128, 256:512],
                                                 lbc2[64:128, :], ALU.mult)
                            vector.maybe_drain_then_inc((sV, 1))
                        C["sV"] += 1
                        lbc_hist = C["sV"]
                        evac_hist[gp] = C["sV"]
                        gseq += 1

                    attn_done = C["sV"]
                    for stt in range(2):
                        for half in range(2):
                            @block.tensor
                            def _(tensor, stt=stt, half=half, need_v=attn_done,
                                  need_x=xres_dma, pfree=py_free):
                                tensor.wait_ge(sV, max(need_v, pfree))
                                tensor.wait_ge(dX, need_x)
                                for c in range(8):
                                    tensor.matmul(py[:, :], attn[:, c, stt * 128:(stt + 1) * 128],
                                                  woS[:, c, half * 512:(half + 1) * 512],
                                                  start=(c == 0), stop=False)
                                tensor.matmul(py[:, :], ones1[0:1, :],
                                              obrow[0:1, half * 512:(half + 1) * 512],
                                              start=False, stop=True).then_inc(sT, 1)
                            C["sT"] += 1

                            @block.vector
                            def _(vector, stt=stt, half=half, need=C["sT"]):
                                vector.wait_ge(sT, need)
                                vector.tensor_tensor(xres[:, stt, half * 512:(half + 1) * 512],
                                                     py[:, :],
                                                     xres[:, stt, half * 512:(half + 1) * 512],
                                                     ALU.add)
                                vector.maybe_drain_then_inc((sV, 1))
                            C["sV"] += 1
                            py_free = C["sV"]

                        @block.sync
                        def _(sync, j=j, stt=stt, need=C["sV"]):
                            sync.wait_ge(sV, need)
                            dst = out_d if attn_only else x2_d
                            sync.dma_start(
                                out=dst[j * 256 + stt * 128:j * 256 + (stt + 1) * 128, :],
                                in_=xres[:, stt, :]).then_inc(dS, 16)
                        C["dS"] += 16
                    oproj_hist = C["sT"]

            p2_dS = C["dS"]

        satt.close()

        if attn_only:
            with nc.Block() as block:
                @block.sync
                def _(sync):
                    sync.wait_ge(dS, p2_dS)
            return nc

        def ar_round(block, src_cols):
            @block.sync
            def _(sync, src_cols=src_cols, need=C["sV"]):
                sync.wait_ge(sV, need)
                for i in range(2):
                    col = src_cols[i] if i < len(src_cols) else src_cols[0]
                    sync.dma_start(out=cci_d[i:i + 1, :].rearrange("one p -> p one"),
                                   in_=col).then_inc(dS, 16)
            C["dS"] += 32

            @block.gpsimd
            def _(gpsimd, need=C["dS"]):
                gpsimd.wait_ge(dS, need)
                gpsimd.collective_compute(
                    "AllReduce", ALU.max, replica_groups=[list(range(8))],
                    ins=[cci_d[0:2, :]], outs=[cco_d[0:2, :]],
                ).then_inc(sC, 1)
            C["sC"] += 1

            @block.sync
            def _(sync, need=C["sC"]):
                sync.wait_ge(sC, need)
                sync.dma_start(out=sc[0:1, 128:256], in_=cco_d[0:1, :]).then_inc(dS, 16)
                sync.dma_start(out=sc[0:1, 256:384], in_=cco_d[1:2, :]).then_inc(dS, 16)
            C["dS"] += 32

            @block.vector
            def _(vector, n=len(src_cols), need=C["dS"]):
                vector.wait_ge(dS, need)
                vector.tensor_reduce(sc[0:1, 48:49], sc[0:1, 128:256], AX.X, ALU.max)
                if n == 2:
                    vector.tensor_reduce(sc[0:1, 49:50], sc[0:1, 256:384], AX.X, ALU.max)
                vector.maybe_drain_then_inc((sV, 1))
            C["sV"] += 1

        def bcast_scalars(block, n, scb_cols):
            @block.sync
            def _(sync, n=n, need=C["sV"]):
                sync.wait_ge(sV, need)
                sync.dma_start(out=scal_d[0:1, 0:n], in_=sc[0:1, 32:32 + n]).then_inc(dS, 16)
            C["dS"] += 16

            @block.sync
            def _(sync, n=n, scb_cols=scb_cols, need=C["dS"]):
                sync.wait_ge(dS, need)
                for i in range(n):
                    sync.dma_start(out=scb[:, scb_cols[i]:scb_cols[i] + 1],
                                   in_=scal_d[0:1, i:i + 1].broadcast_to([128, 1])).then_inc(dS, 16)
            C["dS"] += 16 * n

        with contextlib.ExitStack() as s3o:
            B3 = lambda n, sh, dt: s3o.enter_context(nc.sbuf_tensor(n, sh, dt))
            scrA = B3("scrA", [128, 2048], F32)
            scrB = B3("scrB", [128, 2048], F32)
            sq = B3("sq", [128, 32], F32)
            sqi = B3("sqi", [128, 32], F32)
            pg3 = [s3o.enter_context(nc.psum_tensor(f"pg3{i}", [128, 512], F32)) for i in range(2)]
            ptp3 = [s3o.enter_context(nc.psum_tensor(f"ptp3{i}", [128, 1024], BF16)) for i in range(2)]

            with contextlib.ExitStack() as s3b:
                Xfm = s3b.enter_context(nc.sbuf_tensor("Xfm", [128, 8, NTOK], BF16))
                wgrid = s3b.enter_context(nc.sbuf_tensor("wgrid", [128, 1024], BF16))

                with contextlib.ExitStack() as s3a:
                    xln2 = s3a.enter_context(nc.sbuf_tensor("xln2", [128, 16, 1024], F32))
                    xst2 = s3a.enter_context(nc.sbuf_tensor("xst2", [128, 2, 1024], F32))
                    bns2 = s3a.enter_context(nc.sbuf_tensor("bns2", [128, 12], F32))
                    mv2 = s3a.enter_context(nc.sbuf_tensor("mv2", [128, 4], F32))

                    with nc.Block() as block:
                        @block.scalar
                        def _(scalar):
                            scalar.preload_act_table(AF.Sqrt)

                        x2_free = {0: 0, 1: 0}
                        for t in range(16):
                            par = t % 2

                            @block.sync
                            def _(sync, t=t, par=par, need=(p2_dS if t == 0 else 0),
                                  free=x2_free[par]):
                                if need:
                                    sync.wait_ge(dS, need)
                                if free:
                                    sync.wait_ge(sV, free)
                                sync.dma_start(out=xst2[:, par, :],
                                               in_=x2_d[t * 128:(t + 1) * 128, :]).then_inc(dS, 16)
                            C["dS"] += 16

                            @block.vector
                            def _(vector, par=par, need=C["dS"]):
                                vector.wait_ge(dS, need)
                                vector.bn_stats(bns2[:, 0:6], xst2[:, par, 0:512])
                                vector.bn_stats(bns2[:, 6:12], xst2[:, par, 512:1024])
                                vector.drain()
                                vector.bn_aggr(mv2[:, 0:2], bns2[:, 0:12])
                                vector.drain()
                                vector.tensor_scalar_add(mv2[:, 2:3], mv2[:, 1:2], EPS)
                                vector.maybe_drain_then_inc((sV, 1))
                            C["sV"] += 1

                            @block.scalar
                            def _(scalar, need=C["sV"]):
                                scalar.wait_ge(sV, need)
                                scalar.sqrt(mv2[:, 3:4], mv2[:, 2:3])
                                scalar.maybe_drain_then_inc((sS, 1))
                            C["sS"] += 1

                            @block.vector
                            def _(vector, t=t, par=par, need=C["sS"]):
                                vector.wait_ge(sS, need)
                                vector.reciprocal(mv2[:, 3:4], mv2[:, 3:4])
                                vector.drain()
                                vector.tensor_scalar(xln2[:, t, :], xst2[:, par, :], mv2[:, 0:1],
                                                     mv2[:, 3:4], ALU.subtract, ALU.mult)
                                vector.drain()
                                vector.tensor_reduce(amx[:, t:t + 1], xln2[:, t, :], AX.X, ALU.max,
                                                     apply_absolute_value=True)
                                vector.maybe_drain_then_inc((sV, 1))
                            C["sV"] += 1
                            x2_free[par] = C["sV"]

                        @block.vector
                        def _(vector, need=C["sV"]):
                            vector.wait_ge(sV, need)
                            vector.tensor_reduce(amx[:, 16:17], amx[:, 0:16], AX.X, ALU.max)
                            vector.maybe_drain_then_inc((sV, 1))
                        C["sV"] += 1
                        ar_round(block, [amx[:, 16:17]])

                        @block.vector
                        def _(vector, need=C["sV"]):
                            vector.wait_ge(sV, need)
                            vector.tensor_scalar(sc[0:1, 0:1], sc[0:1, 48:49], QMAXI, 1e-12,
                                                 ALU.mult, ALU.max)
                            vector.drain()
                            vector.reciprocal(sc[0:1, 32:33], sc[0:1, 0:1])
                            vector.maybe_drain_then_inc((sV, 1))
                        C["sV"] += 1
                        bcast_scalars(block, 1, [0])

                        grid_free = 0
                        ptp3_free = {0: 0, 1: 0}
                        for t in range(16):
                            par = t % 2

                            @block.vector
                            def _(vector, t=t, need=C["dS"], tfree=grid_free):
                                vector.wait_ge(dS, need)
                                if tfree:
                                    vector.wait_ge(sT, tfree)
                                vector.tensor_scalar(scrA[:, 0:1024], xln2[:, t, :], scb[:, 0:1],
                                                     MAGIC, ALU.mult, ALU.add)
                                vector.drain()
                                vector.tensor_scalar(wgrid[:, :], scrA[:, 0:1024], MAGIC, None,
                                                     ALU.subtract)
                                vector.maybe_drain_then_inc((sV, 1))
                            C["sV"] += 1

                            @block.tensor
                            def _(tensor, par=par, need=C["sV"], pfree=ptp3_free[par]):
                                tensor.wait_ge(sV, max(need, pfree))
                                for c in range(8):
                                    ins = tensor.transpose(ptp3[par][:, c * 128:(c + 1) * 128],
                                                           wgrid[:, c * 128:(c + 1) * 128], ident[:, :])
                                ins.then_inc(sT, 1)
                            C["sT"] += 1
                            grid_free = C["sT"]

                            @block.vector
                            def _(vector, t=t, par=par, need=C["sT"]):
                                vector.wait_ge(sT, need)
                                vector.tensor_copy(
                                    Xfm[:, :, t * 128:(t + 1) * 128],
                                    ptp3[par][:, :].rearrange("p (c q) -> p c q", c=8))
                                vector.maybe_drain_then_inc((sV, 1))
                            C["sV"] += 1
                            ptp3_free[par] = C["sV"]

                gate = s3b.enter_context(nc.sbuf_tensor("gate", [128, NF, NTOK], BF16))
                scrC = s3b.enter_context(nc.sbuf_tensor("scrC", [128, 2048], F32))
                scrD = s3b.enter_context(nc.sbuf_tensor("scrD", [128, 2048], F32))
                wst = [s3b.enter_context(nc.sbuf_tensor(f"wst{i}", [128, 1024], F32)) for i in range(2)]
                wT = [s3b.enter_context(nc.sbuf_tensor(f"wT{i}", [128, 8, 128], BF16)) for i in range(2)]
                upst = [s3b.enter_context(nc.sbuf_tensor(f"upst{i}", [128, NTOK], BF16)) for i in range(2)]

                with nc.Block() as block:
                    state = {"dq_done": 0, "w_dma": {0: 0, 1: 0},
                             "pg_free": {0: 0, 1: 0}, "up_dma": {0: 0, 1: 0},
                             "ptp3_free": {0: C["sT"], 1: C["sT"]}}

                    chain_state = {"scrD_free": 0}

                    def emit_gate_chain(f):
                        @block.vector
                        def _(vector, f=f, need=C["dS"]):
                            vector.wait_ge(dS, need)
                            vector.tensor_scalar(scrC[:, :], gate[:, f, :], scb[:, 1:2], MAGIC,
                                                 ALU.mult, ALU.add)
                            vector.drain()
                            vector.tensor_scalar(gate[:, f, :], scrC[:, :], MAGIC, None,
                                                 ALU.subtract)
                            vector.maybe_drain_then_inc((sV, 1))
                        C["sV"] += 1

                        @block.scalar
                        def _(scalar, f=f, need=C["sV"], sfree=chain_state["scrD_free"]):
                            scalar.wait_ge(sV, max(need, sfree))
                            scalar.activation(scrD[:, :], gate[:, f, :], AF.Sigmoid,
                                              bias=0.0, scale=scb[:, 2:3])
                            scalar.maybe_drain_then_inc((sS, 1))
                        C["sS"] += 1

                        @block.vector
                        def _(vector, f=f, need=C["sS"]):
                            vector.wait_ge(sS, need)
                            vector.tensor_scalar(scrC[:, :], scrD[:, :], 65536.0, MAGIC,
                                                 ALU.mult, ALU.add)
                            vector.drain()
                            vector.tensor_scalar(scrD[:, :], scrC[:, :], MAGIC, 65535.0,
                                                 ALU.subtract, ALU.min)
                            vector.drain()
                            vector.tensor_tensor(gate[:, f, :], gate[:, f, :], scrD[:, :], ALU.mult)
                            vector.drain()
                            vector.tensor_reduce(amx[:, f:f + 1], gate[:, f, :], AX.X, ALU.max,
                                                 apply_absolute_value=True)
                            vector.maybe_drain_then_inc((sV, 1))
                        C["sV"] += 1
                        chain_state["scrD_free"] = C["sV"]

                    def ffn_mm(which, wsrc_d, amx_base):
                        for f in range(NF):
                            par = f % 2

                            @block.sync
                            def _(sync, f=f, par=par, wsrc_d=wsrc_d, free=state["dq_done"]):
                                if free:
                                    sync.wait_ge(sV, free)
                                sync.dma_start(out=wst[par][:, :],
                                               in_=wsrc_d[f * 128:(f + 1) * 128, :]).then_inc(dS, 16)
                            C["dS"] += 16

                            @block.vector
                            def _(vector, par=par, need=C["dS"], tfree=state["ptp3_free"][par]):
                                vector.wait_ge(dS, need)
                                if tfree:
                                    vector.wait_ge(sT, tfree)
                                w3 = wst[par][:, :].rearrange("p (b c) -> p b c", c=32)
                                a3 = scrA[:, 0:1024].rearrange("p (b c) -> p b c", c=32)
                                vector.tensor_reduce(sq[:, :].rearrange("p (b o) -> p b o", o=1),
                                                     w3, AX.X, ALU.max, apply_absolute_value=True)
                                vector.drain()
                                vector.tensor_scalar(sq[:, :], sq[:, :], 1.0 / 7.0, 1e-12,
                                                     ALU.mult, ALU.max)
                                vector.drain()
                                vector.reciprocal(sqi[:, :], sq[:, :])
                                vector.drain()
                                vector.tensor_tensor(a3, w3, sqi[:, :].rearrange("p (b o) -> p b o", o=1)
                                                     .broadcast_to([128, 32, 32]), ALU.mult)
                                vector.drain()
                                vector.tensor_scalar(scrA[:, 0:1024], scrA[:, 0:1024], MAGIC, MAGIC,
                                                     ALU.add, ALU.subtract)
                                vector.drain()
                                vector.tensor_tensor(wgrid[:, :].rearrange("p (b c) -> p b c", c=32),
                                                     a3, sq[:, :].rearrange("p (b o) -> p b o", o=1)
                                                     .broadcast_to([128, 32, 32]), ALU.mult)
                                vector.maybe_drain_then_inc((sV, 1))
                            C["sV"] += 1

                            @block.tensor
                            def _(tensor, par=par, need=C["sV"], wfree=state["dq_done"]):
                                tensor.wait_ge(sV, need)
                                if wfree:
                                    tensor.wait_ge(sV, wfree)
                                for c in range(8):
                                    ins = tensor.transpose(ptp3[par][:, c * 128:(c + 1) * 128],
                                                           wgrid[:, c * 128:(c + 1) * 128], ident[:, :])
                                ins.then_inc(sT, 1)
                            C["sT"] += 1
                            state["ptp3_free"][par] = C["sT"]

                            @block.vector
                            def _(vector, par=par, need=C["sT"]):
                                vector.wait_ge(sT, need)
                                vector.tensor_copy(wT[par][:, :, :],
                                                   ptp3[par][:, :].rearrange("p (c q) -> p c q", c=8))
                                vector.maybe_drain_then_inc((sV, 1))
                            C["sV"] += 1
                            state["dq_done"] = C["sV"]

                            for sp in range(4):
                                ppar = sp % 2

                                @block.tensor
                                def _(tensor, par=par, sp=sp, ppar=ppar, need=state["dq_done"],
                                      pfree=state["pg_free"][ppar]):
                                    tensor.wait_ge(sV, max(need, pfree))
                                    for c in range(8):
                                        ins = tensor.matmul(pg3[ppar][:, :], wT[par][:, c, :],
                                                            Xfm[:, c, sp * 512:(sp + 1) * 512],
                                                            start=(c == 0), stop=(c == 7))
                                    ins.then_inc(sT, 1)
                                C["sT"] += 1

                                @block.vector
                                def _(vector, f=f, sp=sp, ppar=ppar, which=which, need=C["sT"],
                                      ufree=(C["dS"] if (which == "up" and sp == 0 and state["up_dma"][f % 2]) else 0)):
                                    vector.wait_ge(sT, need)
                                    if which == "gate":
                                        vector.tensor_copy(gate[:, f, sp * 512:(sp + 1) * 512],
                                                           pg3[ppar][:, :])
                                    else:
                                        if ufree:
                                            vector.wait_ge(dS, ufree)
                                        vector.tensor_copy(upst[f % 2][:, sp * 512:(sp + 1) * 512],
                                                           pg3[ppar][:, :])
                                    vector.maybe_drain_then_inc((sV, 1))
                                C["sV"] += 1
                                state["pg_free"][ppar] = C["sV"]

                            if which == "gate":
                                @block.vector
                                def _(vector, f=f, need=C["sV"]):
                                    vector.wait_ge(sV, need)
                                    vector.tensor_reduce(amx[:, amx_base + f:amx_base + f + 1],
                                                         gate[:, f, :], AX.X, ALU.max,
                                                         apply_absolute_value=True)
                                    vector.maybe_drain_then_inc((sV, 1))
                                C["sV"] += 1
                            else:
                                @block.vector
                                def _(vector, f=f, need=C["sV"]):
                                    vector.wait_ge(sV, need)
                                    vector.tensor_reduce(amx[:, amx_base + f:amx_base + f + 1],
                                                         upst[f % 2][:, :], AX.X, ALU.max,
                                                         apply_absolute_value=True)
                                    vector.maybe_drain_then_inc((sV, 1))
                                C["sV"] += 1

                                @block.sync
                                def _(sync, f=f, need=C["sV"]):
                                    sync.wait_ge(sV, need)
                                    sync.dma_start(out=upb_d[f, :, :],
                                                   in_=upst[f % 2][:, :]).then_inc(dS, 16)
                                C["dS"] += 16
                                state["up_dma"][f % 2] = C["dS"]
                                emit_gate_chain(f)

                    ffn_mm("gate", wgt_d, 0)

                    @block.vector
                    def _(vector, need=C["sV"]):
                        vector.wait_ge(sV, need)
                        vector.tensor_reduce(amx[:, 96:97], amx[:, 0:24], AX.X, ALU.max)
                        vector.maybe_drain_then_inc((sV, 1))
                    C["sV"] += 1
                    ar_round(block, [amx[:, 96:97]])

                    @block.vector
                    def _(vector, need=C["sV"]):
                        vector.wait_ge(sV, need)
                        vector.tensor_tensor(sc[0:1, 1:2], sc[0:1, 48:49], sc[0:1, 0:1], ALU.mult)
                        vector.drain()
                        vector.tensor_scalar(sc[0:1, 1:2], sc[0:1, 1:2], QMAXI, 1e-12,
                                             ALU.mult, ALU.max)
                        vector.drain()
                        vector.reciprocal(sc[0:1, 2:3], sc[0:1, 1:2])
                        vector.drain()
                        vector.tensor_tensor(sc[0:1, 32:33], sc[0:1, 0:1], sc[0:1, 2:3], ALU.mult)
                        vector.tensor_copy(sc[0:1, 33:34], sc[0:1, 1:2])
                        vector.maybe_drain_then_inc((sV, 1))
                    C["sV"] += 1
                    bcast_scalars(block, 2, [1, 2])

                    @block.scalar
                    def _(scalar):
                        scalar.preload_act_table(AF.Sigmoid)

                    ffn_mm("up", wup_d, 24)

                    @block.vector
                    def _(vector, need=C["sV"]):
                        vector.wait_ge(sV, need)
                        vector.tensor_reduce(amx[:, 98:99], amx[:, 0:24], AX.X, ALU.max)
                        vector.drain()
                        vector.tensor_reduce(amx[:, 99:100], amx[:, 24:48], AX.X, ALU.max)
                        vector.maybe_drain_then_inc((sV, 1))
                    C["sV"] += 1
                    ar_round(block, [amx[:, 98:99], amx[:, 99:100]])

                    @block.vector
                    def _(vector, need=C["sV"]):
                        vector.wait_ge(sV, need)
                        vector.tensor_scalar(sc[0:1, 3:4], sc[0:1, 1:2], 1.0 / 65536.0, None, ALU.mult)
                        vector.drain()
                        vector.tensor_tensor(sc[0:1, 4:5], sc[0:1, 48:49], sc[0:1, 3:4], ALU.mult)
                        vector.drain()
                        vector.tensor_scalar(sc[0:1, 4:5], sc[0:1, 4:5], QMAXI, 1e-12, ALU.mult, ALU.max)
                        vector.drain()
                        vector.tensor_tensor(sc[0:1, 5:6], sc[0:1, 49:50], sc[0:1, 0:1], ALU.mult)
                        vector.drain()
                        vector.tensor_scalar(sc[0:1, 5:6], sc[0:1, 5:6], QMAXI, 1e-12, ALU.mult, ALU.max)
                        vector.drain()
                        vector.reciprocal(sc[0:1, 6:7], sc[0:1, 4:5])
                        vector.reciprocal(sc[0:1, 7:8], sc[0:1, 5:6])
                        vector.drain()
                        vector.tensor_tensor(sc[0:1, 32:33], sc[0:1, 3:4], sc[0:1, 6:7], ALU.mult)
                        vector.tensor_tensor(sc[0:1, 33:34], sc[0:1, 0:1], sc[0:1, 7:8], ALU.mult)
                        vector.maybe_drain_then_inc((sV, 1))
                    C["sV"] += 1
                    bcast_scalars(block, 2, [3, 4])

                    up_q_free = {0: 0, 1: 0}
                    for f in range(NF):
                        par = f % 2

                        @block.sync
                        def _(sync, f=f, par=par, free=up_q_free[par], bc=C["dS"]):
                            sync.wait_ge(dS, bc)
                            if free:
                                sync.wait_ge(sV, free)
                            sync.dma_start(out=upst[par][:, :], in_=upb_d[f, :, :]).then_inc(dS, 16)
                        C["dS"] += 16

                        @block.vector
                        def _(vector, f=f, par=par, need=C["dS"]):
                            vector.wait_ge(dS, need)
                            vector.tensor_scalar(scrA[:, :], gate[:, f, :], scb[:, 3:4], MAGIC,
                                                 ALU.mult, ALU.add)
                            vector.tensor_scalar(scrB[:, :], upst[par][:, :], scb[:, 4:5], MAGIC,
                                                 ALU.mult, ALU.add)
                            vector.drain()
                            vector.tensor_scalar(scrA[:, :], scrA[:, :], MAGIC, None, ALU.subtract)
                            vector.tensor_scalar(scrB[:, :], scrB[:, :], MAGIC, None, ALU.subtract)
                            vector.drain()
                            vector.tensor_tensor(gate[:, f, :], scrA[:, :], scrB[:, :], ALU.mult)
                            vector.drain()
                            vector.tensor_reduce(amx[:, f:f + 1], gate[:, f, :], AX.X, ALU.max,
                                                 apply_absolute_value=True)
                            vector.maybe_drain_then_inc((sV, 1))
                        C["sV"] += 1
                        up_q_free[par] = C["sV"]

                    @block.vector
                    def _(vector, need=C["sV"]):
                        vector.wait_ge(sV, need)
                        vector.tensor_reduce(amx[:, 100:101], amx[:, 0:24], AX.X, ALU.max)
                        vector.maybe_drain_then_inc((sV, 1))
                    C["sV"] += 1
                    ar_round(block, [amx[:, 100:101]])

                    @block.vector
                    def _(vector, need=C["sV"]):
                        vector.wait_ge(sV, need)
                        vector.tensor_tensor(sc[0:1, 8:9], sc[0:1, 4:5], sc[0:1, 5:6], ALU.mult)
                        vector.drain()
                        vector.tensor_tensor(sc[0:1, 9:10], sc[0:1, 48:49], sc[0:1, 8:9], ALU.mult)
                        vector.drain()
                        vector.tensor_scalar(sc[0:1, 9:10], sc[0:1, 9:10], QMAXI, 1e-12, ALU.mult, ALU.max)
                        vector.drain()
                        vector.reciprocal(sc[0:1, 10:11], sc[0:1, 9:10])
                        vector.drain()
                        vector.tensor_tensor(sc[0:1, 32:33], sc[0:1, 8:9], sc[0:1, 10:11], ALU.mult)
                        vector.tensor_copy(sc[0:1, 33:34], sc[0:1, 9:10])
                        vector.maybe_drain_then_inc((sV, 1))
                    C["sV"] += 1
                    bcast_scalars(block, 2, [5, 6])

                    for f in range(NF):
                        @block.vector
                        def _(vector, f=f, need=C["dS"]):
                            vector.wait_ge(dS, need)
                            vector.tensor_scalar(scrA[:, :], gate[:, f, :], scb[:, 5:6], MAGIC,
                                                 ALU.mult, ALU.add)
                            vector.drain()
                            vector.tensor_scalar(gate[:, f, :], scrA[:, :], MAGIC, None, ALU.subtract)
                            vector.maybe_drain_then_inc((sV, 1))
                        C["sV"] += 1

                        @block.sync
                        def _(sync, f=f, need=C["sV"]):
                            sync.wait_ge(sV, need)
                            sync.dma_start(out=upb_d[f, :, :], in_=gate[:, f, :]).then_inc(dS, 16)
                        C["dS"] += 16
                    oint_dS = C["dS"]

            with contextlib.ExitStack() as s3c:
                wdT = s3c.enter_context(nc.sbuf_tensor("wdT", [128, NF, 1024], BF16))
                ob = [s3c.enter_context(nc.sbuf_tensor(f"ob{i}", [128, NF, 128], BF16)) for i in range(2)]
                wst3 = [s3c.enter_context(nc.sbuf_tensor(f"wst3{i}", [128, I3], F32)) for i in range(2)]
                wdg = [s3c.enter_context(nc.sbuf_tensor(f"wdg{i}", [128, I3], BF16)) for i in range(2)]
                sq3 = s3c.enter_context(nc.sbuf_tensor("sq3", [128, 96], F32))
                sqi3 = s3c.enter_context(nc.sbuf_tensor("sqi3", [128, 96], F32))
                xr3 = s3c.enter_context(nc.sbuf_tensor("xr3", [128, 2, 1024], F32))

                with nc.Block() as block:
                    wst3_free = {0: 0, 1: 0}
                    wdg_free = {0: 0, 1: 0}
                    tposf = {0: 0, 1: 0}
                    for m in range(8):
                        par3 = m % 2

                        @block.sync
                        def _(sync, m=m, par3=par3, free=wst3_free[par3]):
                            if free:
                                sync.wait_ge(sV, free)
                            sync.dma_start(out=wst3[par3][:, :],
                                           in_=wdn_d[m * 128:(m + 1) * 128, :]).then_inc(dS, 16)
                        C["dS"] += 16

                        @block.vector
                        def _(vector, m=m, par3=par3, need=C["dS"], tneed=wdg_free[par3]):
                            vector.wait_ge(dS, need)
                            if tneed:
                                vector.wait_ge(sT, tneed)
                            w3 = wst3[par3][:, :].rearrange("p (b c) -> p b c", c=32)
                            a3 = scrA[:, 0:2048].rearrange("p (b c) -> p b c", c=32)
                            a3b = scrB[:, 0:1024].rearrange("p (b c) -> p b c", c=32)
                            vector.tensor_reduce(sq3[:, :].rearrange("p (b o) -> p b o", o=1),
                                                 w3, AX.X, ALU.max, apply_absolute_value=True)
                            vector.drain()
                            vector.tensor_scalar(sq3[:, :], sq3[:, :], 1.0 / 7.0, 1e-12,
                                                 ALU.mult, ALU.max)
                            vector.drain()
                            vector.reciprocal(sqi3[:, :], sq3[:, :])
                            vector.drain()
                            vector.tensor_tensor(a3, w3[:, 0:64, :],
                                                 sqi3[:, 0:64].rearrange("p (b o) -> p b o", o=1)
                                                 .broadcast_to([128, 64, 32]), ALU.mult)
                            vector.tensor_tensor(a3b, w3[:, 64:96, :],
                                                 sqi3[:, 64:96].rearrange("p (b o) -> p b o", o=1)
                                                 .broadcast_to([128, 32, 32]), ALU.mult)
                            vector.drain()
                            vector.tensor_scalar(scrA[:, 0:2048], scrA[:, 0:2048], MAGIC, MAGIC,
                                                 ALU.add, ALU.subtract)
                            vector.tensor_scalar(scrB[:, 0:1024], scrB[:, 0:1024], MAGIC, MAGIC,
                                                 ALU.add, ALU.subtract)
                            vector.drain()
                            vector.tensor_tensor(wdg[par3][:, 0:2048].rearrange("p (b c) -> p b c", c=32),
                                                 a3, sq3[:, 0:64].rearrange("p (b o) -> p b o", o=1)
                                                 .broadcast_to([128, 64, 32]), ALU.mult)
                            vector.tensor_tensor(wdg[par3][:, 2048:3072].rearrange("p (b c) -> p b c", c=32),
                                                 a3b, sq3[:, 64:96].rearrange("p (b o) -> p b o", o=1)
                                                 .broadcast_to([128, 32, 32]), ALU.mult)
                            vector.maybe_drain_then_inc((sV, 1))
                        C["sV"] += 1
                        wst3_free[par3] = C["sV"]

                        for fg in range(3):
                            par = fg % 2

                            @block.tensor
                            def _(tensor, m=m, fg=fg, par=par, par3=par3, need=C["sV"],
                                  pfree=tposf[par]):
                                tensor.wait_ge(sV, max(need, pfree))
                                for c in range(8):
                                    f = fg * 8 + c
                                    ins = tensor.transpose(ptp3[par][:, c * 128:(c + 1) * 128],
                                                           wdg[par3][:, f * 128:(f + 1) * 128], ident[:, :])
                                ins.then_inc(sT, 1)
                            C["sT"] += 1
                            if fg == 2:
                                wdg_free[par3] = C["sT"]

                            @block.vector
                            def _(vector, m=m, fg=fg, par=par, need=C["sT"]):
                                vector.wait_ge(sT, need)
                                vector.tensor_copy(
                                    wdT[:, fg * 8:(fg + 1) * 8, m * 128:(m + 1) * 128],
                                    ptp3[par][:, :].rearrange("p (c q) -> p c q", c=8))
                                vector.maybe_drain_then_inc((sV, 1))
                            C["sV"] += 1
                            tposf[par] = C["sV"]

                    xr_free = {0: 0, 1: 0}
                    pgf_free = {0: 0, 1: 0}
                    ob_free = {0: 0, 1: 0}
                    for t in range(16):
                        par = t % 2

                        @block.sync
                        def _(sync, t=t, par=par,
                              free=(C["dS"] if (xr_free[par] or t < 2) else 0),
                              obfree=ob_free[par]):
                            if free:
                                sync.wait_ge(dS, free)
                            if obfree:
                                sync.wait_ge(sT, obfree)
                            sync.dma_start(out=xr3[:, par, :],
                                           in_=x2_d[t * 128:(t + 1) * 128, :]).then_inc(dS, 16)
                            sync.dma_start(
                                out=ob[par][:, :, :],
                                in_=upb_d[:, :, t * 128:(t + 1) * 128].rearrange("f p t -> p f t"),
                            ).then_inc(dS, 16)
                        C["dS"] += 32
                        xr_dma = C["dS"]

                        for half in range(2):
                            @block.tensor
                            def _(tensor, t=t, par=par, half=half, need=max(tposf.values()),
                                  pfree=pgf_free[half], obd=xr_dma):
                                tensor.wait_ge(sV, max(need, pfree))
                                tensor.wait_ge(dS, obd)
                                for f in range(NF):
                                    ins = tensor.matmul(pg3[half][:, :],
                                                        ob[par][:, f, :],
                                                        wdT[:, f, half * 512:(half + 1) * 512],
                                                        start=(f == 0), stop=(f == NF - 1))
                                ins.then_inc(sT, 1)
                            C["sT"] += 1
                            ob_free[par] = C["sT"]

                            @block.vector
                            def _(vector, t=t, par=par, half=half, need=C["sT"], xd=xr_dma):
                                vector.wait_ge(sT, need)
                                vector.wait_ge(dS, xd)
                                vector.scalar_tensor_tensor(
                                    xr3[:, par, half * 512:(half + 1) * 512],
                                    pg3[half][:, :], scb[:, 6:7],
                                    xr3[:, par, half * 512:(half + 1) * 512],
                                    ALU.mult, ALU.add)
                                vector.maybe_drain_then_inc((sV, 1))
                            C["sV"] += 1
                            pgf_free[half] = C["sV"]

                        @block.sync
                        def _(sync, t=t, par=par, need=C["sV"]):
                            sync.wait_ge(sV, need)
                            sync.dma_start(out=out_d[t * 128:(t + 1) * 128, :],
                                           in_=xr3[:, par, :]).then_inc(dS, 16)
                        C["dS"] += 16
                        xr_free[par] = C["dS"]

                    @block.sync
                    def _(sync, need=C["dS"]):
                        sync.wait_ge(dS, need)

    return nc


_NC_CACHE = {}


def kernel(L0_attn_x, L0_attn_Q_w, L0_attn_Q_b, L0_attn_K_w, L0_attn_K_b,
           L0_attn_V_w, L0_attn_V_b, L0_attn_O_w, L0_attn_O_b,
           L0_ffn_w_gate, L0_ffn_w_up, L0_ffn_w_down):
    x = np.asarray(L0_attn_x, np.float32)
    bf = ml_dtypes.bfloat16
    wq = np.ascontiguousarray(np.asarray(L0_attn_Q_w, np.float32).transpose(1, 0, 2).reshape(D, D))
    wk = np.ascontiguousarray(np.asarray(L0_attn_K_w, np.float32).transpose(1, 0, 2).reshape(D, D))
    wv = np.ascontiguousarray(np.asarray(L0_attn_V_w, np.float32).transpose(1, 0, 2).reshape(D, D))
    wo = np.ascontiguousarray(np.asarray(L0_attn_O_w, np.float32).reshape(D, D))
    qbv = np.ascontiguousarray(np.asarray(L0_attn_Q_b, np.float32).reshape(8, 128).T)
    kbv = np.ascontiguousarray(np.asarray(L0_attn_K_b, np.float32).reshape(8, 128).T)
    vbv = np.ascontiguousarray(np.asarray(L0_attn_V_b, np.float32).reshape(1, D))
    obv = np.ascontiguousarray(np.asarray(L0_attn_O_b, np.float32).reshape(1, D))
    wup = np.ascontiguousarray(np.asarray(L0_ffn_w_up, np.float32))
    wgt = np.ascontiguousarray(np.asarray(L0_ffn_w_gate, np.float32))
    wdn = np.ascontiguousarray(np.asarray(L0_ffn_w_down, np.float32))
    ident = np.eye(128, dtype=np.float32).astype(bf)
    selm = np.zeros((128, 256), np.float32)
    selm[64, 0:128] = 1.0
    selm[63, 128:256] = 1.0

    if "nc" not in _NC_CACHE:
        _NC_CACHE["nc"] = build_nc()
    nc = _NC_CACHE["nc"]

    in_maps = []
    row_idx = []
    for c in range(8):
        b, p = c // 2, c % 2
        rows = np.concatenate([np.arange(512 * j + 256 * p, 512 * j + 256 * p + 256)
                               for j in range(NJ)])
        row_idx.append((b, rows))
        tt = np.arange(4)[:, None, None]
        kk = np.arange(128)[None, :, None]
        qq = np.arange(256)[None, None, :]
        keep = (128 * tt + kk) <= (256 * p + qq)
        msk = keep.astype(np.float32).astype(bf)
        in_maps.append({
            "xq": np.ascontiguousarray(x[b][rows]),
            "xkv": np.ascontiguousarray(x[b]),
            "wq": wq, "wk": wk, "wv": wv, "wo": wo,
            "qbias": qbv, "kbias": kbv, "vbias": vbv, "obias": obv,
            "masks": msk, "ident": ident, "selm": selm,
            "wup": wup, "wgt": wgt, "wdn": wdn,
        })

    res = run_bass_kernel_spmd(nc, in_maps, core_ids=list(range(8)))
    out = np.empty((B, S, D), np.float32)
    for c in range(8):
        b, rows = row_idx[c]
        out[b][rows] = np.asarray(res.results[c]["out"], np.float32)
    return out
